# version 06
# baseline (speedup 1.0000x reference)
import numpy as np
from contextlib import ExitStack
import concourse.bass as bass
import concourse.mybir as mybir
from concourse.bass_utils import run_bass_kernel_spmd

F32 = mybir.dt.float32
BF16 = mybir.dt.bfloat16
I32 = mybir.dt.int32
AF = mybir.ActivationFunctionType
ALU = mybir.AluOpType
AX = mybir.AxisListType

NCORES = 8
D = 1024
NH = 16
DH = 64
S = 4096
T = 2048
NBT = 16
NC = 8
INNER = 4096
ENC = 2048
NEG = np.float32(np.finfo(np.float32).min)

ENGINES = ('sync', 'tensor', 'vector', 'scalar', 'gpsimd')


class Prog:

    def __init__(self, nc):
        self.nc = nc
        self.steps = {e: [] for e in ENGINES}
        self.cnt = {('p:' + e): 0 for e in ENGINES}
        self.cnt['cc'] = 0

    def _fix(self, deps):
        out = []
        for k, v in deps:
            if k.startswith('d:'):
                out.append((k, self.cnt.get(k, v)))
            else:
                out.append((k, v))
        return tuple(out)

    def do(self, eng, fn, deps=()):
        self.cnt['p:' + eng] += 1
        self.steps[eng].append(('op', self._fix(deps), fn, 'p:' + eng))
        return ('p:' + eng, self.cnt['p:' + eng])

    def dma(self, eng, fn, deps=(), n=1, ch=0):
        key = 'd:%s:%d' % (eng, ch)
        self.cnt[key] = self.cnt.get(key, 0) + 16 * n
        self.steps[eng].append(('dma', self._fix(deps), fn, key))
        return (key, self.cnt[key])

    def cc(self, fn, deps=()):
        self.cnt['cc'] += 1
        self.steps['gpsimd'].append(('op', self._fix(deps), fn, 'cc'))
        return ('cc', self.cnt['cc'])

    def wait(self, eng, deps):
        self.steps[eng].append(('wait', self._fix(deps), None, None))

    def now(self, eng):
        return ('p:' + eng, self.cnt['p:' + eng])

    def barrier(self):
        toks = [(k, v) for k, v in self.cnt.items()
                if v and k != 'cc']
        for e in ENGINES:
            self.wait(e, toks)

    def emit(self, block, sems):
        for eng in ENGINES:
            self._emit_engine(block, eng, sems)

    def _emit_engine(self, block, eng, sems):
        steps = self.steps[eng]
        if not steps:
            return
        dec = getattr(block, eng)
        last_wait = {}
        prog = self
        own = 'p:' + eng

        def body(h):
            counts = {k: 0 for k in prog.cnt}
            ecount = 0
            for kind, deps, fn, inckey in steps:
                for sem_key, val in deps:
                    if sem_key == own:
                        if val > ecount - 8 and val > last_wait.get(own, -1):
                            h.wait_ge(sems[own], val)
                            last_wait[own] = val
                        continue
                    if last_wait.get(sem_key, -1) >= val:
                        continue
                    h.wait_ge(sems[sem_key], val)
                    last_wait[sem_key] = val
                if inckey == own:
                    ecount += 1
                if kind == 'wait':
                    continue
                if kind == 'dma':
                    instrs = fn(h)
                    if not isinstance(instrs, (list, tuple)):
                        instrs = [instrs]
                    for i in instrs:
                        i.then_inc(sems[inckey], 16)
                    counts[inckey] += 16 * len(instrs)
                else:
                    instr = fn(h)
                    assert instr is not None, f"step on {eng} returned None"
                    instr.then_inc(sems[inckey], 1)
                    counts[inckey] += 1

        dec(body)


def ts(i, n):
    return slice(i * n, (i + 1) * n)



def _stripe_rows(par):
    idx = []
    for bt in range(NBT):
        j = 2 * bt + par
        idx.append(np.arange(j * 128, (j + 1) * 128))
    return np.concatenate(idx)


def _mask_self(par):
    m = np.zeros((32, 128, 128), dtype=np.float32)
    for kb in range(32):
        t0 = kb // 2
        qblk = 2 * t0 + par
        k_tok = kb * 128 + np.arange(128)[:, None]
        q_tok = qblk * 128 + np.arange(128)[None, :]
        m[kb] = np.where(q_tok >= k_tok, 0.0, NEG)
    return m


def _mask_cross(par):
    m = np.zeros((16, 2, 128, 128), dtype=np.float32)
    for kb in range(16):
        t0 = max(0, (kb - 1) // 2)
        for j in range(2):
            t = t0 + j
            if t > 15:
                m[kb, j] = 0.0
                continue
            qblk = 2 * t + par
            k_tok = kb * 128 + np.arange(128)[:, None]
            q_tok = qblk * 128 + np.arange(128)[None, :]
            m[kb, j] = np.where(k_tok <= q_tok + 2, 0.0, NEG)
    return m


def kernel(**inp):
    full_x = np.ascontiguousarray(inp['L0_attn_x'])
    enc = np.ascontiguousarray(inp['L1_attn_encoder_hidden_states'][0])
    scorer = np.ascontiguousarray(inp['L1_attn_scorer_logits'][0])

    weight_names = [k for k in inp.keys()
                    if k not in ('L0_attn_x', 'L1_attn_encoder_hidden_states',
                                 'L1_attn_scorer_logits')]

    nc, wshapes = build_program()

    in_maps = []
    for cid in range(NCORES):
        b = cid // 2
        par = cid & 1
        rows = _stripe_rows(par)
        m = {
            'x0': np.ascontiguousarray(full_x[b][rows]),
            'enc_slice': np.ascontiguousarray(enc[cid * 256:(cid + 1) * 256]),
            'scorer_cols': np.ascontiguousarray(scorer.reshape(16, 128).T),
            'scorer_row': np.ascontiguousarray(scorer.reshape(1, 2048)),
            'ident': np.eye(128, dtype=np.float32),
            'mask_self': _mask_self(par),
            'mask_cross': _mask_cross(par),
        }
        for k in weight_names:
            m[k] = np.ascontiguousarray(inp[k].reshape(wshapes[k]))
        in_maps.append(m)

    res = run_bass_kernel_spmd(nc, in_maps, core_ids=list(range(NCORES)))

    out = np.empty((4, 4096, 1024), dtype=np.float32)
    for cid in range(NCORES):
        b = cid // 2
        par = cid & 1
        rows = _stripe_rows(par)
        out[b][rows] = res.results[cid]['y']
    return out



DBG = None


def build_program():
    nc = bass.Bass("TRN2", target_bir_lowering=False)
    wshapes = {}

    def win(name, shape):
        wshapes[name] = tuple(shape)
        return nc.dram_tensor(name, list(shape), F32, kind="ExternalInput").ap()

    x0 = nc.dram_tensor('x0', [T, D], F32, kind="ExternalInput").ap()
    enc_slice = nc.dram_tensor('enc_slice', [256, D], F32, kind="ExternalInput").ap()
    scorer_cols = nc.dram_tensor('scorer_cols', [128, 16], F32, kind="ExternalInput").ap()
    scorer_row = nc.dram_tensor('scorer_row', [1, ENC], F32, kind="ExternalInput").ap()
    ident_in = nc.dram_tensor('ident', [128, 128], F32, kind="ExternalInput").ap()
    mask_self_d = nc.dram_tensor('mask_self', [32, 128, 128], F32, kind="ExternalInput").ap()
    mask_cross_d = nc.dram_tensor('mask_cross', [16, 2, 128, 128], F32, kind="ExternalInput").ap()

    ns0 = win('L0_attn_norm_scale', (8, 128))
    wq0 = win('L0_attn_wq', (D, D))
    wk0 = win('L0_attn_wk', (D, D))
    wv0 = win('L0_attn_wv', (D, D))
    wo0 = win('L0_attn_w_out', (D, D))
    bo0 = win('L0_attn_b_out', (1, D))
    w1 = win('L0_ffn_w1', (INNER, D))
    b1 = win('L0_ffn_b1', (1, INNER))
    w2 = win('L0_ffn_w2', (D, INNER))
    b2 = win('L0_ffn_b2', (1, D))
    ln1g = win('L1_attn_ln1_g', (8, 128))
    ln1b = win('L1_attn_ln1_b', (8, 128))
    caw = win('L1_attn_c_attn_w', (D, 3 * D))
    cab = win('L1_attn_c_attn_b', (24, 128))
    apw = win('L1_attn_attn_proj_w', (D, D))
    apb = win('L1_attn_attn_proj_b', (1, D))
    lnxg = win('L1_attn_lnx_g', (8, 128))
    lnxb = win('L1_attn_lnx_b', (8, 128))
    qaw = win('L1_attn_q_attn_w', (D, D))
    qab = win('L1_attn_q_attn_b', (8, 128))
    xkw = win('L1_attn_x_kv_w', (D, 2 * D))
    xkb = win('L1_attn_x_kv_b', (16, 128))
    xpw = win('L1_attn_x_proj_w', (D, D))
    xpb = win('L1_attn_x_proj_b', (1, D))
    ln2g = win('L1_attn_ln2_g', (8, 128))
    ln2b = win('L1_attn_ln2_b', (8, 128))
    fcw = win('L1_attn_fc_w', (D, INNER))
    fcb = win('L1_attn_fc_b', (32, 128))
    mpw = win('L1_attn_mlp_proj_w', (INNER, D))
    mpb = win('L1_attn_mlp_proj_b', (1, D))
    wg = win('L1_ffn_w_gate', (INNER, D))
    wu = win('L1_ffn_w_up', (INNER, D))
    wd = win('L1_ffn_w_down', (D, INNER))

    y = nc.dram_tensor('y', [T, D], F32, kind="ExternalOutput").ap()
    dbg_out = None
    if DBG is not None:
        dbg_out = nc.dram_tensor('dbgout', [DBG[1]], F32, kind="ExternalOutput")

    KV_SHARD = 2 * T * D + 32
    kv0_in_t = nc.dram_tensor('kv0_in', [KV_SHARD], F32)
    kv0_out_t = nc.dram_tensor('kv0_out', [2 * KV_SHARD], F32)
    kv1_in_t = nc.dram_tensor('kv1_in', [KV_SHARD], F32)
    kv1_out_t = nc.dram_tensor('kv1_out', [2 * KV_SHARD], F32)
    KVX_SHARD = 2 * 256 * D + 16
    kvx_in_t = nc.dram_tensor('kvx_in', [KVX_SHARD], F32)
    kvx_out_t = nc.dram_tensor('kvx_out', [8 * KVX_SHARD], F32)
    qt0_t = nc.dram_tensor('qt0', [D, T], F32)
    qt1_t = nc.dram_tensor('qt1', [D, T], F32)
    qtx_t = nc.dram_tensor('qtx', [D, T], F32)
    g0_t = nc.dram_tensor('g0', [T, D], F32)
    g1_t = nc.dram_tensor('g1', [T, D], F32)
    x1_t = nc.dram_tensor('x1sp', [T, D], F32)
    hsp_t = nc.dram_tensor('hsp', [INNER, T], F32)
    xnfd_t = nc.dram_tensor('xnfd', [128 * 16384], F32)
    w1q_t = nc.dram_tensor('w1qT', [D, INNER], BF16)
    w2q_t = nc.dram_tensor('w2qT', [INNER, D], BF16)
    wgq_t = nc.dram_tensor('wgqT', [D, INNER], BF16)
    wuq_t = nc.dram_tensor('wuqT', [D, INNER], BF16)
    wdq_t = nc.dram_tensor('wdqT', [INNER, D], BF16)
    dbg_t = nc.dram_tensor('dbg', [128, 16384], F32, kind="ExternalOutput").ap() if DBG else None

    P = Prog(nc)

    with ExitStack() as st:
        ec = st.enter_context
        X = ec(nc.sbuf_tensor("X", [128, 16384], F32))
        FM = ec(nc.sbuf_tensor("FM", [128, 16384], F32))
        S1 = ec(nc.sbuf_tensor("S1", [128, 4096], F32))
        PTS = ec(nc.sbuf_tensor("PTS", [128, 4096], F32))
        QH = ec(nc.sbuf_tensor("QH", [128, 2048], F32))
        KVB = ec(nc.sbuf_tensor("KVB", [128, 1288], F32))
        XQT = ec(nc.sbuf_tensor("XQT", [128, 4096], BF16))
        QBF = ec(nc.sbuf_tensor("QBF", [128, 4096], BF16))
        WS = ec(nc.sbuf_tensor("WS", [128, 4096], F32))
        STATS = ec(nc.sbuf_tensor("STATS", [128, 400], F32))
        ONE = ec(nc.sbuf_tensor("ONE", [128, 128], F32))
        IDT = ec(nc.sbuf_tensor("IDT", [128, 128], F32))
        IDB = ec(nc.sbuf_tensor("IDB", [128, 128], BF16))
        PS = ec(nc.psum_tensor("PS", [128, 4096], F32))

        sems = {}
        for e in ENGINES:
            sems['p:' + e] = ec(nc.semaphore('sp_' + e))
        for e in ('sync', 'gpsimd'):
            for chn in range(16):
                sems['d:%s:%d' % (e, chn)] = ec(nc.semaphore('sd_%s_%d' % (e, chn)))
        sems['cc'] = ec(nc.semaphore('scc'))

        def LD(dst, src, deps=(), eng='sync', ch=0):
            return P.dma(eng, lambda h, d=dst, s=src: h.dma_start(out=d, in_=s), deps, ch=ch)

        def ST(dst, src, deps=(), eng='gpsimd'):
            return P.dma(eng, lambda h, d=dst, s=src: h.dma_start(out=d, in_=s), deps)

        def OP(eng, fn, deps=()):
            return P.do(eng, fn, deps)

        bank_tok = {}
        def bank_dep(b):
            t = bank_tok.get(b)
            return [t] if t else []

        def set_bank(b, tok):
            bank_tok[b] = tok

        def bank(b, n=512):
            return PS[:, b * 512: b * 512 + n]

        def mm_group(out_aps, pairs, deps=(), start=True, stop=True, eng='tensor'):
            def f(h, out_aps=out_aps, pairs=pairs, start=start, stop=stop):
                last = None
                n = len(pairs)
                for i, (lt, rh) in enumerate(pairs):
                    o = out_aps[i] if isinstance(out_aps, list) else out_aps
                    last = h.matmul(o, lt, rh, start=(start and i == 0),
                                    stop=(stop and i == n - 1))
                return last
            return P.do('tensor', f, deps)

        def bias_mm(out_ap, bias_row, deps=(), start=False, stop=True, base=0):
            def f(h, o=out_ap, br=bias_row, base=base, start=start, stop=stop):
                return h.matmul(o, ONE[base:base + 1, 0:128], br,
                                start=start, stop=stop)
            return P.do('tensor', f, deps)

        ev_cycle = [0]
        def evac_eng():
            ev_cycle[0] ^= 1
            return 'vector' if ev_cycle[0] else 'scalar'

        def COPY(eng, dst, src, deps=()):
            if eng == 'scalar':
                return OP('scalar', lambda h, d=dst, s=src: h.activation(d, s, AF.Copy), deps)
            return OP(eng, lambda h, d=dst, s=src: h.tensor_copy(d, s), deps)

        def trans128(src, dst, b, deps=(), scale=None, bias_ap=None, bf=False):
            pst = PS[:, b * 512: b * 512 + 128]
            if bf:
                pst = pst.bitcast(BF16)[:, 0:128]
            t1 = OP('tensor', lambda h, s=src, p=pst: h.transpose(p, s, (IDB if bf else IDT)[:]),
                    list(deps) + bank_dep(b))
            e = evac_eng()
            if scale is not None:
                def f(h, d=dst, p=pst, sc=scale, bi=bias_ap):
                    if bi is not None:
                        return h.tensor_scalar(d, p, sc, bi, ALU.mult, ALU.add)
                    return h.tensor_scalar(d, p, sc, None, ALU.mult)
                t2 = OP('vector', f, [t1])
            else:
                t2 = COPY(e, dst, pst, [t1])
            set_bank(b, t2)
            return t2

        def prep_pass1(wd, R, C, scol, chA, chB, deps=()):
            acc = STATS[:, scol + 2:scol + 3]
            tacc = OP('vector', lambda h, o=acc: h.memset(o, 0.0), deps)
            cons = {}
            np_ = (R // 128) * (C // 1024)
            for p_ in range(np_):
                j, q = divmod(p_, C // 1024)
                sl = PTS[:, ts(p_ & 1, 1024)]
                tl = P.dma('sync', lambda h, d=sl, sr=wd[ts(j, 128), ts(q, 1024)]:
                           h.dma_start(out=d, in_=sr),
                           ([cons[p_ & 1]] if (p_ & 1) in cons else list(deps)),
                           ch=chA if (p_ & 1) == 0 else chB)
                tr2 = OP('vector', lambda h, o=STATS[:, scol + 4 + (p_ & 1):scol + 5 + (p_ & 1)],
                         s=sl: h.reduce_sum(o, s, axis=AX.X, apply_absolute_value=True), [tl])
                cons[p_ & 1] = tr2
                tacc = OP('vector', lambda h, o=acc, a=STATS[:, scol + 4 + (p_ & 1):scol + 5 + (p_ & 1)]:
                          h.tensor_add(o, o, a), [tr2, tacc])
            return [tacc, cons.get(0), cons.get(1)]

        OP('vector', lambda h: h.memset(ONE[:], 1.0))
        t_id = LD(IDT[:], ident_in)
        t_sc = LD(STATS[0:128, 100:116], scorer_cols)
        t_x = [LD(X[:, ts(bt, 1024)], x0[ts(bt, 128), :]) for bt in range(NBT)]
        OP('vector', lambda h: h.tensor_copy(IDB[:], IDT[:]), [t_x[-1]])
        p1_toks = {}
        p1_toks[288] = prep_pass1(w1, INNER, D, 288, 1, 2)
        p1_toks[296] = prep_pass1(w2, D, INNER, 296, 1, 2, deps=[p1_toks[288][0]])
        p1_toks[304] = prep_pass1(wg, INNER, D, 304, 1, 2, deps=[p1_toks[296][0]])
        p1_toks[312] = prep_pass1(wu, INNER, D, 312, 1, 2, deps=[p1_toks[304][0]])
        p1_toks[320] = prep_pass1(wd, D, INNER, 320, 1, 2, deps=[p1_toks[312][0]])

        MU0, MU1 = 0, 32
        BSCR = 192
        BV_L0, BV_SELF, BV_CROSS = 203, 204, 205
        CBV = 208
        QSCR = 240

        O2 = 76
        to2a = OP('vector', lambda h: h.memset(STATS[:, O2:O2 + 2], 0.0))
        to2b = OP('vector', lambda h: h.memset(STATS[0:64, O2:O2 + 1], 1.0), [to2a])
        OP('vector', lambda h: h.memset(STATS[64:128, O2 + 1:O2 + 2], 1.0), [to2b])

        stats_chain = {0: None, 1: None}

        def ln_stats(src, mu_ap, r_ap, w, eps, rms, scr, deps, tset=0):
            dd = list(deps) + ([stats_chain[tset]] if stats_chain[tset] else [])
            b0 = 64 + 6 * tset
            sq = STATS[:, b0:b0 + 1]
            sm = STATS[:, b0 + 1:b0 + 2]
            m2 = STATS[:, b0 + 2:b0 + 3]
            vr = STATS[:, b0 + 3:b0 + 4]
            sd = STATS[:, b0 + 4:b0 + 5]
            t1 = OP('scalar', lambda h, s=src, o=scr, a=sq:
                    h.activation(o, s, AF.Square, accum_out=a), dd)
            if rms:
                t2 = OP('vector', lambda h, a=sq, o=vr, iw=1.0 / w, e=eps:
                        h.tensor_scalar(o, a, iw, e, ALU.mult, ALU.add), [t1])
            else:
                t0 = OP('vector', lambda h, s=src, o=sm:
                        h.reduce_sum(o, s, axis=AX.X), dd)
                tmu = OP('vector', lambda h, a=sm, o=mu_ap, iw=1.0 / w:
                         h.tensor_scalar(o, a, iw, None, ALU.mult), [t0])
                tm2 = OP('vector', lambda h, a=mu_ap, o=m2:
                         h.tensor_mul(o, a, a), [tmu])
                tm2 = OP('vector', lambda h, o=m2, e=eps:
                         h.tensor_scalar(o, o, e, None, ALU.subtract), [tm2])
                t2 = OP('vector', lambda h, a=sq, b=m2, o=vr, iw=1.0 / w:
                        h.scalar_tensor_tensor(o, a, iw, b, ALU.mult, ALU.subtract),
                        [t1, tm2])
            t3 = OP('scalar', lambda h, a=vr, o=sd:
                    h.activation(o, a, AF.Sqrt), [t2])
            t4 = OP('vector', lambda h, a=sd, o=r_ap: h.reciprocal(o, a), [t3])
            stats_chain[tset] = t4
            return t4

        def quant_act(src, out_bf, inv_ap, w, scrA, scrB, deps, rm_ap=None, tset=0):
            b0 = 186 if tset else 197
            am = STATS[:, b0:b0 + 1]
            cl = STATS[:, b0 + 1:b0 + 2]
            cf = STATS[:, b0 + 2:b0 + 3]
            dd = list(deps) + ([stats_chain[tset]] if stats_chain[tset] else [])
            t0 = OP('vector', lambda h, s=src, o=am:
                    h.reduce_max(o, s, axis=AX.X, apply_absolute_value=True), dd)
            if rm_ap is not None:
                t0 = OP('vector', lambda h, o=am, a=rm_ap: h.tensor_mul(o, o, a), [t0])
            t1 = OP('vector', lambda h, a=am, o=cl:
                    h.tensor_scalar(o, a, 1e-5, None, ALU.max), [t0])
            t2 = OP('vector', lambda h, a=cl, o=inv_ap:
                    h.tensor_scalar(o, a, 1.0 / 127.0, None, ALU.mult), [t1])
            t3 = OP('vector', lambda h, a=cl, o=cf: h.reciprocal(o, a), [t1])
            t4 = OP('vector', lambda h, o=cf: h.tensor_scalar(o, o, 127.0, None, ALU.mult), [t3])
            if rm_ap is not None:
                t4 = OP('vector', lambda h, o=cf, a=rm_ap: h.tensor_mul(o, o, a), [t4])
            stats_chain[tset] = t4
            iw = scrA.bitcast(I32)
            tV = OP('vector', lambda h, s=src, o=scrB, a=cf:
                    h.tensor_scalar(o, s, a, 0.5, ALU.mult, ALU.add), [t4] + list(deps))
            ti = OP('vector', lambda h, s=scrB, o=iw: h.tensor_copy(o, s), [tV])
            tf = OP('vector', lambda h, s=iw, o=scrA: h.tensor_copy(o, s), [ti])
            td = OP('vector', lambda h, a=scrA, b=scrB, o=scrB:
                    h.tensor_tensor(o, a, b, ALU.is_gt), [tf])
            tr = OP('vector', lambda h, a=scrA, b=scrB, o=scrA:
                    h.tensor_sub(o, a, b), [td])
            tq = OP('vector', lambda h, a=scrA, o=out_bf:
                    h.tensor_scalar(o, a, -128.0, 127.0, ALU.max, ALU.min), [tr])
            return tq, ('p:vector', P.cnt['p:vector'])

        def prep_weight(wd, R, C, out_t, scol, deps=(), acc_deps=()):
            acc = STATS[:, scol + 2:scol + 3]
            tacc = None
            for d_ in acc_deps:
                if d_ is not None:
                    tacc = d_
            tacc = tacc or ('p:vector', 0)
            tmm = OP('tensor', lambda h: h.matmul(PS[0:1, 7 * 512:7 * 512 + 1],
                                                  acc, ONE[:, 0:1],
                                                  start=True, stop=True),
                     [tacc] + list(acc_deps[1:]) + bank_dep(7))
            tot = STATS[0:1, scol + 5:scol + 6]
            te = OP('vector', lambda h, o=tot: h.tensor_copy(o, PS[0:1, 7 * 512:7 * 512 + 1]), [tmm])
            set_bank(7, te)
            mcl = STATS[0:1, scol + 6:scol + 7]
            t1 = OP('vector', lambda h, a=tot, o=mcl, iw=1.0 / (R * C):
                    h.tensor_scalar(o, a, iw, 1e-5, ALU.mult, ALU.max), [te])
            sS = STATS[0:1, scol + 7:scol + 8]
            t2 = OP('vector', lambda h, a=mcl, o=sS: h.reciprocal(o, a), [t1])
            tb = OP('tensor', lambda h, a=sS: h.matmul(PS[:, 7 * 512:7 * 512 + 1],
                                                       ONE[0:1, 0:128], a,
                                                       start=True, stop=True),
                    [t2] + bank_dep(7))
            sbc = STATS[:, scol:scol + 1]
            tsb = OP('vector', lambda h, o=sbc: h.tensor_copy(o, PS[:, 7 * 512:7 * 512 + 1]), [tb])
            set_bank(7, tsb)
            tsb = OP('vector', lambda h, o=STATS[:, scol + 1:scol + 2], a=sbc:
                     h.reciprocal(o, a), [tsb])
            tprev = [tsb] + list(deps)
            slot_use2 = {}
            pi_ = 0
            for jg in range(R // 512):
                for cg in range(C // 1024):
                    par2 = pi_ & 1
                    pi_ += 1
                    LB = WS if par2 == 0 else PTS
                    tls = []
                    for jj in range(4):
                        j = jg * 4 + jj
                        tls.append(LD(LB[:, ts(jj, 1024)], wd[ts(j, 128), ts(cg, 1024)],
                                      slot_use2.get(par2, tprev), ch=8 + par2 * 7))
                    tq4 = []
                    for jj in range(4):
                        tm1 = OP('vector', lambda h, s=LB[:, ts(jj, 1024)], o=S1[:, ts(jj, 1024)], a=sbc:
                                 h.tensor_scalar(o, s, a, 0.5, ALU.mult, ALU.is_ge),
                                 [tls[-1]] + tprev[-1:])
                        tm2 = OP('vector', lambda h, s=LB[:, ts(jj, 1024)], o=FM[:, ts(jj, 1024)], a=sbc:
                                 h.tensor_scalar(o, s, a, -0.5, ALU.mult, ALU.is_lt),
                                 [tls[-1]] + tprev[-1:])
                        tq = OP('vector', lambda h, a=S1[:, ts(jj, 1024)], b=FM[:, ts(jj, 1024)],
                                o=QBF[:, ts(jj, 1024)]: h.tensor_sub(o, a, b), [tm1, tm2])
                        tq4.append(tq)
                    slot_use2[par2] = [tq4[-1]]
                    tprev = []
                    for cs in range(8):
                        tev = []
                        for jj in range(4):
                            tev.append(trans128(QBF[:, jj * 1024 + cs * 128: jj * 1024 + cs * 128 + 128],
                                                XQT[:, ts(jj, 128)], 6 + (cs & 1),
                                                deps=[tq4[jj]] + bank_dep(77), bf=True))
                        row = cg * 1024 + cs * 128
                        tst = ST(bass.AP(out_t, row * R + jg * 512, [[R, 128], [1, 512]]),
                                 XQT[:, 0:512], tev)
                        bank_tok[77] = tst
                        tprev.append(tst)
            return tprev

        def load_cols(dram2d, row0, n, col0, scale=None, deps=()):
            lts = [LD(STATS[:, col0 + c:col0 + c + 1], dram2d[row0 + c:row0 + c + 1, :],
                      deps, ch=9) for c in range(n)]
            if scale is None:
                return [lts[-1]]
            toks = []
            for c in range(n):
                toks.append(OP('vector', lambda h, o=STATS[:, col0 + c:col0 + c + 1], s=scale:
                               h.tensor_scalar(o, o, s, None, ALU.mult), [lts[-1]]))
            return [toks[-1]]

        def norm_probe(psum_ap, ncol, scr, deps):
            n = psum_ap.shape[-1]
            t1 = OP('scalar', lambda h, s=psum_ap, o=scr[:, 0:n]:
                    h.activation(o, s, AF.Square), list(deps) + bank_dep(79))
            t2 = OP('tensor', lambda h, s=scr[:, 0:n], nn=n:
                    h.matmul(PS[0:2, 7 * 512:7 * 512 + nn], STATS[:, O2:O2 + 2], s,
                             start=True, stop=True), [t1] + bank_dep(7))
            bank_tok[79] = t2
            tmp = STATS[0:2, BSCR + 9:BSCR + 10]
            t3 = OP('vector', lambda h, o=tmp, nn=n:
                    h.reduce_max(o, PS[0:2, 7 * 512:7 * 512 + nn], axis=AX.X), [t2])
            set_bank(7, t3)
            t4 = OP('vector', lambda h, o=STATS[0:2, ncol:ncol + 1], a=tmp:
                    h.tensor_max(o, o, a), [t3])
            return t4

        def norm_fin(ncol, out11, deps):
            t1 = P.dma('gpsimd', lambda h: h.dma_start(
                out=STATS[0:1, out11 + 1:out11 + 3], in_=STATS[0:2, ncol:ncol + 1]), deps)
            t2 = OP('vector', lambda h: h.reduce_max(
                STATS[0:1, out11:out11 + 1], STATS[0:1, out11 + 1:out11 + 3], axis=AX.X), [t1])
            return t2

        def proj_fm(wdram, wcol0, nhe, fm_cols, writer, scale_col=None,
                    bias_col0=None, ncol=None, nscr=None, deps=(), aux=()):
            tw_prev = list(deps)
            slot_use = {}
            for hc in range(nhe // 128):
                sb_ = 2048 + (hc & 1) * 1024
                chn = 8 + (hc & 1) * 7
                tl = []
                for c in range(NC):
                    tl.append(LD(WS[:, sb_ + c * 128: sb_ + (c + 1) * 128],
                                 wdram[ts(c, 128), wcol0 + hc * 128: wcol0 + (hc + 1) * 128],
                                 slot_use.get(hc & 1, tw_prev), ch=chn))
                tws = [tl[-1]]
                if scale_col is not None:
                    tws = []
                    for c in range(NC):
                        tws.append(OP('vector', lambda h, o=WS[:, sb_ + c * 128:sb_ + (c + 1) * 128],
                                      s=STATS[:, scale_col + c:scale_col + c + 1]:
                                      h.tensor_scalar(o, o, s, None, ALU.mult),
                                      [tl[-1]] + list(aux)))
                ntt = (fm_cols[1] - fm_cols[0]) // (NC * 512)
                tw_prev = []
                for tt_ in range(ntt):
                    b = 4 + (tt_ & 1)
                    pairs = [(WS[:, sb_ + c * 128: sb_ + (c + 1) * 128],
                              FM[:, fm_cols[0] + c * (fm_cols[1] - fm_cols[0]) // NC + tt_ * 512:
                                 fm_cols[0] + c * (fm_cols[1] - fm_cols[0]) // NC + (tt_ + 1) * 512])
                             for c in range(NC)]
                    tm = mm_group(bank(b), pairs, tws + bank_dep(b))
                    pa = bank(b)
                    dd = [tm]
                    if bias_col0 is not None:
                        tb = OP('vector', lambda h, p=pa, bc=STATS[:, bias_col0 + hc:bias_col0 + hc + 1]:
                                h.tensor_scalar(p, p, bc, None, ALU.add), [tm] + list(aux))
                        dd = [tb]
                    if ncol is not None:
                        dd.append(norm_probe(pa, ncol, nscr, dd))
                    tvw = writer(hc, tt_, pa, dd)
                    set_bank(b, tvw)
                    tw_prev.append(tvw)
                slot_use[hc & 1] = list(tw_prev)
            return tw_prev

        def proj_tm(rhs_dram, rcol0, nout, lhsT_fn, nbt_, writer, bias_row=None,
                    scale_col=None, deps=(), aux=()):
            tw_prev = list(deps)
            for nt in range(nout // 512):
                tl = []
                for c in range(NC):
                    tl.append(LD(WS[:, ts(c, 512)],
                                 rhs_dram[ts(c, 128), rcol0 + nt * 512: rcol0 + (nt + 1) * 512],
                                 tw_prev, ch=8))
                tws = [tl[-1]]
                if scale_col is not None:
                    tws = []
                    for c in range(NC):
                        tws.append(OP('vector', lambda h, o=WS[:, ts(c, 512)],
                                      s=STATS[:, scale_col + c:scale_col + c + 1]:
                                      h.tensor_scalar(o, o, s, None, ALU.mult),
                                      [tl[-1]] + list(aux)))
                tbr = []
                bcol = 3072 + (nt & 1) * 512
                if bias_row is not None:
                    tbr = [LD(PTS[0:1, bcol:bcol + 512],
                              bias_row[0:1, nt * 512:(nt + 1) * 512],
                              tw_prev + bank_dep(81 + (nt & 1)), ch=9)]
                tw_prev = []
                for btl in range(nbt_):
                    b = 4 + (btl & 1)
                    pairs = [(lhsT_fn(c, btl), WS[:, ts(c, 512)]) for c in range(NC)]
                    tm = mm_group(bank(b), pairs, tws + bank_dep(b),
                                  stop=(bias_row is None))
                    if bias_row is not None:
                        tm = bias_mm(bank(b), PTS[0:1, bcol:bcol + 512], [tm] + tbr)
                        bank_tok[81 + (nt & 1)] = tm
                    tvw = writer(btl, nt, bank(b), [tm])
                    set_bank(b, tvw)
                    tw_prev.append(tvw)
            return tw_prev

        for bt in range(NBT):
            slot = bt & 1
            xs = X[:, ts(bt, 1024)]
            xn = S1[:, ts(slot, 1024)]
            tst = ln_stats(xs, STATS[:, MU0 + 2 * bt:MU0 + 2 * bt + 1],
                           STATS[:, MU0 + 2 * bt + 1:MU0 + 2 * bt + 2],
                           D, 1e-5, False, S1[:, 2048:3072], [t_x[bt]])
            txn = OP('vector', lambda h, s=xs, o=xn,
                     mu=STATS[:, MU0 + 2 * bt:MU0 + 2 * bt + 1],
                     r=STATS[:, MU0 + 2 * bt + 1:MU0 + 2 * bt + 2]:
                     h.tensor_scalar(o, s, mu, r, ALU.subtract, ALU.mult),
                     [tst] + bank_dep(90 + slot))
            tevs = []
            for c in range(NC):
                tevs.append(trans128(xn[:, ts(c, 128)], FM[:, c * 2048 + bt * 128:c * 2048 + (bt + 1) * 128],
                                     (c & 1), deps=[txn]))
            bank_tok[90 + slot] = tevs[-1]

        P.barrier()
        if DBG is not None and DBG[0] == 'xnfd1':
            ST(bass.AP(xnfd_t, 0, [[16384, 128], [1, 16384]]), FM[:, :])
            P.barrier()

        t_enc = [LD(S1[:, 2048 + i * 1024: 2048 + (i + 1) * 1024], enc_slice[ts(i, 128), :])
                 for i in range(2)]
        tenc_ev = []
        for i in range(2):
            for c in range(NC):
                tenc_ev.append(trans128(S1[:, 2048 + i * 1024 + c * 128: 2048 + i * 1024 + (c + 1) * 128],
                               S1[:, c * 256 + i * 128: c * 256 + (i + 1) * 128],
                               (c & 1), deps=[t_enc[i]]))
        for _c in range(160, 166):
            OP('vector', lambda h, c=_c: h.memset(STATS[0:2, c:c + 1], 0.0))
        tpad = OP('vector', lambda h: h.memset(STATS[0:1, 120:156], 0.0))
        txkb = load_cols(xkb, 0, 8, 256)

        kx_done = []

        def kx_writer(hc, tt_, pa, dd):
            st_ = QH[:, ts(tt_ & 1, 512)][:, 0:256]
            te = COPY(evac_eng(), st_, pa, dd)
            tw = ST(bass.AP(kvx_in_t, hc * 128 * 256, [[256, 128], [1, 256]]), st_, [te])
            kx_done.append(tw)
            return tw

        tw_prev = []
        for hc in range(8):
            tws = []
            for c in range(NC):
                t = LD(WS[:, 2048 + c * 128:2048 + (c + 1) * 128],
                       xkw[ts(c, 128), hc * 128:(hc + 1) * 128], tw_prev, ch=8)
                tws.append(t)
            b = 4 + (hc & 1)
            pairs = [(WS[:, 2048 + c * 128:2048 + (c + 1) * 128], S1[:, ts(c, 256)])
                     for c in range(NC)]
            tm = mm_group(bank(b, 256), pairs, tws + bank_dep(b) + tenc_ev)
            tb = OP('vector', lambda h, p=bank(b, 256), bc=STATS[:, 256 + hc:257 + hc]:
                    h.tensor_scalar(p, p, bc, None, ALU.add), [tm] + txkb)
            tn = norm_probe(bank(b, 256), 160, QH[:, 1024:1536], [tb])
            tw = kx_writer(hc, hc, bank(b, 256), [tb, tn])
            set_bank(b, tw)
            tw_prev = [tw]
        tnf = norm_fin(160, 168, [P.now('vector')])
        tnc = OP('vector', lambda h: h.tensor_copy(STATS[0:1, 120:121],
                                                   STATS[0:1, 168:169]), [tnf, tpad])
        kx_done.append(ST(bass.AP(kvx_in_t, 2 * 256 * D, [[1, 16], [1, 1]]),
                          STATS[0:1, 120:136], [tnc]))
        xkbv = bass.AP(xkb.tensor, 8 * 128, [[1024, 1], [1, 1024]])
        tvb = LD(PTS[0:1, 3072:4096], xkbv, ch=9)

        def vx_writer(btl, nt, pa, dd):
            st_ = QH[:, ts(btl & 1, 512)]
            te = COPY(evac_eng(), st_, pa, dd)
            tw = ST(bass.AP(kvx_in_t, 2 * 256 * D // 2 + btl * 128 * D + nt * 512,
                            [[D, 128], [1, 512]]), st_, [te])
            kx_done.append(tw)
            return tw

        tw_prev = [tw]
        for nt in range(2):
            tws = [LD(WS[:, ts(c, 512)], xkw[ts(c, 128), D + nt * 512:D + (nt + 1) * 512],
                      tw_prev, ch=8) for c in range(NC)]
            tw_prev = []
            for btl in range(2):
                b = 4 + (btl & 1)
                pairs = [(S1[:, c * 256 + btl * 128: c * 256 + (btl + 1) * 128], WS[:, ts(c, 512)])
                         for c in range(NC)]
                tm = mm_group(bank(b), pairs, tws + bank_dep(b) + tenc_ev, stop=False)
                tm = bias_mm(bank(b), PTS[0:1, 3072 + nt * 512:3072 + (nt + 1) * 512], [tm, tvb])
                tw = vx_writer(btl, nt, bank(b), [tm])
                set_bank(b, tw)
                tw_prev.append(tw)
        tvx_done = list(tw_prev)

        t_cc3 = P.cc(lambda h: h.collective_compute(
            "AllGather", ALU.bypass, replica_groups=[list(range(8))],
            ins=[kvx_in_t.ap().opt()], outs=[kvx_out_t.ap().opt()]), kx_done)

        tns = load_cols(ns0, 0, 8, 272)
        for c in range(8):
            OP('vector', lambda h, o=STATS[:, 280 + c:281 + c], s=STATS[:, 272 + c:273 + c]:
               h.tensor_scalar(o, s, 0.125, None, ALU.mult), tns)

        kv0_done = []

        def mk_kv_writer(kv_t, done):
            def wr(hc, tt_, pa, dd):
                st_ = S1[:, 3072 + (tt_ & 1) * 512: 3072 + ((tt_ & 1) + 1) * 512]
                te = COPY(evac_eng(), st_, pa, dd + bank_dep(92 + (tt_ & 1)))
                tw = ST(bass.AP(kv_t, hc * 128 * T + tt_ * 512, [[T, 128], [1, 512]]), st_, [te])
                bank_tok[92 + (tt_ & 1)] = tw
                done.append(tw)
                return tw
            return wr

        def mk_vwriter(kv_t, done):
            def wr(btl, nt, pa, dd):
                st_ = S1[:, 3072 + (btl & 1) * 512: 3072 + ((btl & 1) + 1) * 512]
                te = COPY(evac_eng(), st_, pa, dd + bank_dep(92 + (btl & 1)))
                tw = ST(bass.AP(kv_t, T * D + btl * 128 * D + nt * 512, [[D, 128], [1, 512]]), st_, [te])
                bank_tok[92 + (btl & 1)] = tw
                done.append(tw)
                return tw
            return wr

        tk0 = proj_fm(wk0, 0, D, (0, 16384), mk_kv_writer(kv0_in_t, kv0_done),
                      scale_col=272, ncol=161, nscr=S1[:, 2048:2560], aux=tns,
                      deps=tvx_done)
        tnf = norm_fin(161, 169, [P.now('vector')])
        tnc0 = OP('vector', lambda h: h.tensor_copy(STATS[0:1, 121:122],
                                                    STATS[0:1, 169:170]), [tnf, tpad])
        kv0_done.append(ST(bass.AP(kv0_in_t, 2 * T * D, [[1, 32], [1, 1]]),
                           STATS[0:1, 121:153], [tnc0]))
        tv0 = proj_tm(wv0, 0, D, lambda c, btl: FM[:, c * 2048 + btl * 128: c * 2048 + (btl + 1) * 128],
                      NBT, mk_vwriter(kv0_in_t, kv0_done), scale_col=272, deps=tk0, aux=tns)
        t_cc1 = P.cc(lambda h: h.collective_compute(
            "AllGather", ALU.bypass, replica_groups=[[0, 1], [2, 3], [4, 5], [6, 7]],
            ins=[kv0_in_t.ap().opt()], outs=[kv0_out_t.ap().opt()]), kv0_done)

        def mk_qt_writer(qt_t):
            def wr(hc, tt_, pa, dd):
                st_ = S1[:, 3072 + (tt_ & 1) * 512: 3072 + ((tt_ & 1) + 1) * 512]
                te = COPY(evac_eng(), st_, pa, dd + bank_dep(92 + (tt_ & 1)))
                tw = ST(bass.AP(qt_t, hc * 128 * T + tt_ * 512, [[T, 128], [1, 512]]), st_, [te])
                bank_tok[92 + (tt_ & 1)] = tw
                return tw
            return wr

        proj_fm(wq0, 0, D, (0, 16384), mk_qt_writer(qt0_t),
                scale_col=280, ncol=162, nscr=S1[:, 2048:2560], deps=tv0,
                aux=[P.now('vector')])
        norm_fin(162, 170, [P.now('vector')])

        P.barrier()

        if DBG is not None and DBG[0] == 'xnfd':
            ST(bass.AP(xnfd_t, 0, [[16384, 128], [1, 16384]]), FM[:, :])
            P.barrier()
        if DBG is not None and DBG[0] == 'qt0e':
            ST(bass.AP(xnfd_t, 0, [[2048, 1024], [1, 2048]]),
               bass.AP(qt0_t, 0, [[2048, 1024], [1, 2048]]))
            P.barrier()
        tp = prep_weight(w1, INNER, D, w1q_t, 288, acc_deps=p1_toks[288])
        tp = prep_weight(w2, D, INNER, w2q_t, 296, deps=tp, acc_deps=p1_toks[296])
        tp = prep_weight(wg, INNER, D, wgq_t, 304, deps=tp, acc_deps=p1_toks[304])
        tp = prep_weight(wu, INNER, D, wuq_t, 312, deps=tp, acc_deps=p1_toks[312])
        prep_weight(wd, D, INNER, wdq_t, 320, deps=tp, acc_deps=p1_toks[320])
        P.barrier()

        def mk_bound(mq_col, mk_col, out_bv_col, extra_col=None):
            a = STATS[0:1, BSCR:BSCR + 1]
            b_ = STATS[0:1, BSCR + 1:BSCR + 2]
            t1 = OP('scalar', lambda h, s=STATS[0:1, mq_col:mq_col + 1], o=a:
                    h.activation(o, s, AF.Sqrt), [P.now('vector'), P.now('gpsimd')])
            t2 = OP('scalar', lambda h, s=STATS[0:1, mk_col:mk_col + 1], o=b_:
                    h.activation(o, s, AF.Sqrt), [P.now('vector'), P.now('gpsimd')])
            t3 = OP('vector', lambda h, o=a: h.tensor_mul(o, a, b_), [t1, t2])
            if extra_col is not None:
                t3 = OP('vector', lambda h, o=a, e=STATS[0:1, extra_col:extra_col + 1]:
                        h.tensor_add(o, o, e), [t3])
            t4 = OP('vector', lambda h, o=a: h.tensor_scalar(o, a, -1.0, None, ALU.mult), [t3])
            t5 = OP('tensor', lambda h, s=a: h.matmul(PS[:, 7 * 512:7 * 512 + 1],
                                                      ONE[0:1, 0:128], s,
                                                      start=True, stop=True),
                    [t4] + bank_dep(7))
            t6 = OP('vector', lambda h, o=STATS[:, out_bv_col:out_bv_col + 1]:
                    h.tensor_copy(o, PS[:, 7 * 512:7 * 512 + 1]), [t5])
            set_bank(7, t6)
            return t6

        def attention(kind, kv_t, shard, qt_t, wo_d, bias_row_d, bvec_fn, epi, dep0,
                      bdeps=()):
            nkb = 32 if kind in ('l0', 'self') else 16
            KVB_t = KVB[:, :].tensor
            NSL = 3

            def k_ap(h_, kb):
                if kind == 'cross':
                    off = (kb >> 1) * KVX_SHARD + h_ * 64 * 256 + (kb & 1) * 128
                    return bass.AP(kv_t, off, [[256, 64], [1, 128]])
                off = (kb & 1) * shard + h_ * 64 * T + (kb >> 1) * 128
                return bass.AP(kv_t, off, [[T, 64], [1, 128]])

            def v_ap(h_, kb):
                if kind == 'cross':
                    off = (kb >> 1) * KVX_SHARD + 256 * D + (kb & 1) * 128 * D + h_ * 64
                    return bass.AP(kv_t, off, [[D, 128], [1, 64]])
                off = (kb & 1) * shard + T * D + (kb >> 1) * 128 * D + h_ * 64
                return bass.AP(kv_t, off, [[D, 128], [1, 64]])

            def kb_spec(qg):
                out = []
                for kb in range(nkb):
                    masks = []
                    if kind == 'l0':
                        out.append((kb, 0, masks))
                        continue
                    if kind == 'self':
                        if qg == 0 and kb >= 16:
                            continue
                        t0l = kb // 2 - 8 * qg
                        c0 = max(0, t0l) * 128
                        if 0 <= t0l <= 7:
                            masks.append((bass.AP(mask_self_d.tensor, kb * 16384, [[128, 128], [1, 128]]), t0l * 128))
                        out.append((kb, c0, masks))
                    else:
                        t0 = max(0, (kb - 1) // 2)
                        c0 = max(0, t0 - 8 * qg) * 128
                        for j in range(2):
                            tl = t0 + j - 8 * qg
                            if 0 <= tl <= 7:
                                masks.append((bass.AP(mask_cross_d.tensor, (kb * 2 + j) * 16384, [[128, 128], [1, 128]]), tl * 128))
                        out.append((kb, c0, masks))
                return out

            tv_ones = []
            for s_ in range(2):
                for l_ in range(2):
                    c = 1024 + s_ * 130 + l_ * 65 + 64
                    tv_ones.append(OP('vector', lambda h, c=c: h.memset(KVB[:, c:c + 1], 1.0)))

            def k_gap(h_, kb0):
                if kind == 'cross':
                    off = (kb0 >> 1) * KVX_SHARD + h_ * 64 * 256
                    return bass.AP(kv_t, off, [[256, 64], [128, 2], [1, 128]])
                off = h_ * 64 * T + (kb0 >> 1) * 128
                return bass.AP(kv_t, off, [[T, 64], [shard, 2], [1, 128]])

            def v_gap(h_, kb0):
                if kind == 'cross':
                    off = (kb0 >> 1) * KVX_SHARD + 256 * D + (kb0 & 1) * 128 * D + h_ * 64
                    return bass.AP(kv_t, off, [[D, 128], [128 * D, 2], [1, 64]])
                off = T * D + (kb0 >> 1) * 128 * D + h_ * 64
                return bass.AP(kv_t, off, [[D, 128], [shard, 2], [1, 64]])
            ep_toks = []
            for qg in range(2):
                t_q = [None]
                if ep_toks:
                    slot_S = dict.fromkeys(range(NSL), ep_toks[-1])
                else:
                    slot_S = {}
                slot_PT = dict.fromkeys(range(NSL), dep0[0]) if dep0 else {}
                slot_K = dict.fromkeys(range(NSL), dep0[0]) if dep0 else {}
                mask_tok = {}
                a_dep = bank_dep(0) + bank_dep(1) + tv_ones
                for h_ in range(NH):
                    tq = P.dma('sync', lambda h, d=QH[0:64, 0:1024],
                               sr=bass.AP(qt_t, h_ * 64 * T + qg * 1024, [[T, 64], [1, 1024]]):
                               h.dma_start(out=d, in_=sr),
                               ([t_q[0]] if t_q[0] else []) + list(dep0), ch=7)
                    spec = kb_spec(qg)
                    last_b0 = max((i for i, (kb, c0, m) in enumerate(spec) if c0 < 512),
                                  default=None)
                    pend = [None]
                    last_S = [None]

                    def flush_av():
                        if pend[0] is not None:
                            f, dd, sl = pend[0]
                            pend[0] = None
                            tok = P.do('tensor', f, dd)
                            slot_PT[sl] = tok
                            for gs_ in list(gV):
                                gV[gs_] = (gV[gs_][0], tok)
                            return tok
                        return None

                    gK = {}
                    gV = {}
                    for idx, (kb, c0, masks) in enumerate(spec):
                        slot = idx % NSL
                        mslot = idx & 1
                        if idx % 2 == 0:
                            gslot = (idx // 4) & 1
                            half = (idx // 2) & 1
                            tkg = P.dma('sync', lambda h,
                                        d=bass.AP(KVB_t, gslot * 512 + half * 256,
                                                  [[1288, 64], [128, 2], [1, 128]]),
                                        sr=k_gap(h_, kb):
                                        h.dma_start(out=d, in_=sr),
                                        ([gK[gslot][1]] if gslot in gK and gK[gslot][1]
                                         else list(dep0)), ch=1 + gslot)
                            gK[gslot] = (tkg, gK[gslot][1] if gslot in gK else None)
                        if idx % 2 == 0:
                            vgs = (idx // 2) & 1
                            tvg = P.dma('sync', lambda h,
                                        d=bass.AP(KVB_t, 1024 + vgs * 130, [[1288, 128], [65, 2], [1, 64]]),
                                        sr=v_gap(h_, kb):
                                        h.dma_start(out=d, in_=sr),
                                        ([gV[vgs][1]] if vgs in gV and gV[vgs][1]
                                         else list(dep0)), ch=4 + vgs)
                            gV[vgs] = (tvg, None)
                        gslot = (idx // 4) & 1
                        vgs = (idx // 2) & 1
                        lane = idx % 4
                        kap = KVB[0:64, gslot * 512 + lane * 128: gslot * 512 + (lane + 1) * 128]
                        vap = KVB[:, 1024 + vgs * 130 + (idx % 2) * 65: 1024 + vgs * 130 + (idx % 2 + 1) * 65]
                        tk = gK[gslot][0]
                        tv = gV[vgs][0]
                        tms = []
                        for mi, (mad, mcol) in enumerate(masks):
                            tms.append(P.dma('sync', lambda h,
                                             d=S1[:, (mslot * 2 + mi) * 128:
                                                   (mslot * 2 + mi + 1) * 128],
                                             sr=mad: h.dma_start(out=d, in_=sr),
                                             ([mask_tok[mslot]] if mslot in mask_tok else []),
                                             ch=12 + mslot))
                        flush_av()
                        sb = 2 + 2 * slot
                        segs = ([(sb, c0, 512)] if c0 < 512 else []) + \
                               [(sb + 1, max(c0, 512), 1024)]
                        sdeps = [tk, tq] + ([slot_S[slot]] if slot in slot_S else [])
                        def s_fn(h, segs=segs, kap=kap):
                            last = None
                            for (b_, a0, a1) in segs:
                                last = h.matmul(PS[:, b_ * 512 + (a0 % 512):b_ * 512 + (a0 % 512) + (a1 - a0)],
                                                kap, QH[0:64, a0:a1], start=True, stop=True)
                            return last
                        tS = P.do('tensor', s_fn, sdeps)
                        last_S[0] = tS
                        slot_K[slot] = tS
                        gK[gslot] = (gK[gslot][0], tS)
                        tlast = tS
                        for mi, (mad, mcol) in enumerate(masks):
                            msl = S1[:, (mslot * 2 + mi) * 128: (mslot * 2 + mi + 1) * 128]
                            b_ = sb if mcol < 512 else sb + 1
                            pa = PS[:, b_ * 512 + (mcol % 512): b_ * 512 + (mcol % 512) + 128]
                            tlast = OP('vector', lambda h, p=pa, m=msl:
                                       h.tensor_add(p, p, m), [tlast, tms[mi]])
                            mask_tok[mslot] = tlast
                        bv = bvec_fn(kb)
                        def e_fn(h, slot=slot, c0=c0, bv=bv, sb=sb):
                            return h.activation(PTS[:, slot * 1024 + c0:(slot + 1) * 1024],
                                                PS[:, sb * 512 + c0: sb * 512 + 1024],
                                                AF.Exp, bias=bv)
                        tE = P.do('scalar', e_fn,
                                  [tlast] + list(bdeps) +
                                  ([slot_PT[slot]] if slot in slot_PT else []))
                        slot_S[slot] = tE
                        first = (idx == 0)
                        lastk = (idx == len(spec) - 1)
                        stop0 = (idx == last_b0)
                        def av_fn(h, slot=slot, c0=c0, first=first, lastk=lastk,
                                  stop0=stop0, vap=vap):
                            last = None
                            for (b_, a0, a1) in ([(0, c0, 512)] if c0 < 512 else []) + [(1, max(c0, 512), 1024)]:
                                last = h.matmul(PS[0:65, b_ * 512 + (a0 % 512):b_ * 512 + (a0 % 512) + (a1 - a0)],
                                                vap,
                                                PTS[:, slot * 1024 + a0:slot * 1024 + a1],
                                                start=first,
                                                stop=(stop0 if b_ == 0 else lastk))
                            return last
                        pend[0] = (av_fn, [tE, tv] + (a_dep if first else []), slot)

                        def _upd(gslot=gslot):
                            pass
                        prev_pend_slot = gslot
                    tAV = flush_av()
                    trc = OP('vector', lambda h: h.reciprocal(QH[64:65, 1024:2048],
                                                              PS[64:65, 0:1024]), [tAV])
                    def bc_fn(h):
                        h.matmul(bank(6), ONE[64:65, 0:128], QH[64:65, 1024:1536],
                                 start=True, stop=True)
                        return h.matmul(bank(7), ONE[64:65, 0:128], QH[64:65, 1536:2048],
                                        start=True, stop=True)
                    tbc = P.do('tensor', bc_fn, [trc] + bank_dep(6) + bank_dep(7) +
                               [t for t in slot_S.values()])
                    tbe = OP('vector', lambda h: h.tensor_copy(QH[:, 1024:2048],
                                                               PS[:, 6 * 512:8 * 512]), [tbc])
                    set_bank(6, tbe)
                    set_bank(7, tbe)
                    dst = FM[(h_ % 2) * 64:(h_ % 2) * 64 + 64, (h_ // 2) * 1024:(h_ // 2 + 1) * 1024]
                    tnm = OP('vector', lambda h, d=dst: h.tensor_tensor(
                        d, PS[0:64, 0:1024], QH[0:64, 1024:2048], ALU.mult), [tbe])
                    a_dep = [tnm]
                    bank_tok[0] = tnm
                    bank_tok[1] = tnm
                    t_q[0] = last_S[0]
                for Dt in range(2):
                    tws = [LD(WS[:, ts(c, 512)], wo_d[ts(c, 128), Dt * 512:(Dt + 1) * 512],
                              bank_dep(95), ch=8)
                           for c in range(NC)]
                    tbr = LD(PTS[0:1, 3072 + Dt * 512:3072 + (Dt + 1) * 512],
                             bias_row_d[0:1, Dt * 512:(Dt + 1) * 512], bank_dep(95), ch=9)
                    for btl in range(8):
                        bt = qg * 8 + btl
                        b_ = 6 + (btl & 1)
                        pairs = [(FM[:, c * 1024 + btl * 128: c * 1024 + (btl + 1) * 128],
                                  WS[:, ts(c, 512)]) for c in range(NC)]
                        tm = mm_group(bank(b_), pairs,
                                      tws + [tbr] + bank_dep(b_) + [tnm], stop=False)
                        tm = bias_mm(bank(b_), PTS[0:1, 3072 + Dt * 512:3072 + (Dt + 1) * 512], [tm])
                        tE2 = epi(bt, Dt, bank(b_), [tm])
                        set_bank(b_, tE2)
                        ep_toks.append(tE2)
                    bank_tok[95] = ep_toks[-1]
            return ep_toks

        WSB = WS[:, :].bitcast(BF16)
        R2C, INV1, SC1, INV2, SC2 = 243, 246, 250, 232, 236
        def bcast_row(dst, row_d, n, deps=()):
            toks = []
            for i in range(n // 512):
                tb = LD(QH[0:1, 0:512], row_d[0:1, ts(i, 512)], list(deps) + bank_dep(97), ch=9)
                tm = OP('tensor', lambda h, s=QH[0:1, 0:512]:
                        h.matmul(bank(7), ONE[0:1, 0:128], s, start=True, stop=True),
                        [tb] + bank_dep(7))
                te = COPY(evac_eng(), dst[:, ts(i, 512)], bank(7), [tm])
                set_bank(7, te)
                bank_tok[97] = tm
                toks.append(te)
            return toks

        def l0ffn():
            tb2 = bcast_row(QH[:, 1024:2048], b2, D)
            st_g0 = []
            for g in range(4):
                txq = []
                for bt_l in range(4):
                    bt = g * 4 + bt_l
                    par = bt_l & 1
                    xs = X[:, ts(bt, 1024)]
                    xn = S1[:, ts(par, 1024)]
                    dd = bank_dep(68 + par)
                    txn = OP('vector', lambda h, s=xs, o=xn,
                             mu=STATS[:, MU0 + 2 * bt:MU0 + 2 * bt + 1],
                             r=STATS[:, MU0 + 2 * bt + 1:MU0 + 2 * bt + 2]:
                             h.tensor_scalar(o, s, mu, r, ALU.subtract, ALU.mult), dd)
                    trs = ln_stats(xn, STATS[:, BSCR:BSCR + 1],
                                   STATS[:, R2C + 2 * par:R2C + 2 * par + 1],
                                   D, 1e-6, True, S1[:, 2048 + 1024 * par:3072 + 1024 * par],
                                   [txn], tset=par)
                    tq, _ = quant_act(xn, QBF[:, ts(bt_l, 1024)],
                                      STATS[:, INV1 + bt_l:INV1 + bt_l + 1], D,
                                      PTS[:, ts(par, 1024)],
                                      PTS[:, 2048 + par * 1024:3072 + par * 1024],
                                      [txn, trs], rm_ap=STATS[:, R2C + 2 * par:R2C + 2 * par + 1],
                                      tset=par)
                    bank_tok[68 + par] = tq
                    for c in range(NC):
                        t = trans128(QBF[:, bt_l * 1024 + c * 128: bt_l * 1024 + (c + 1) * 128],
                                     XQT[:, bt_l * 1024 + c * 128: bt_l * 1024 + (c + 1) * 128],
                                     (c & 1), deps=[tq], bf=True)
                    txq.append(t)
                    OP('vector', lambda h, o=STATS[:, SC1 + bt_l:SC1 + bt_l + 1],
                       a=STATS[:, INV1 + bt_l:INV1 + bt_l + 1], b_=STATS[:, 289:290]:
                       h.tensor_mul(o, a, b_), [tq])
                tb1 = bcast_row(PTS[:, 0:4096], b1, INNER, txq)
                tprev = []
                for jt in range(8):
                    tws = [LD(WSB[:, ts(c, 512)],
                              bass.AP(w1q_t, c * 128 * INNER + jt * 512, [[INNER, 128], [1, 512]]),
                              bank_dep(94), ch=8) for c in range(NC)]
                    tprev = []
                    for bt_l in range(4):
                        b_ = 4 + (bt_l & 1)
                        pairs = [(XQT[:, bt_l * 1024 + c * 128: bt_l * 1024 + (c + 1) * 128],
                                  WSB[:, ts(c, 512)]) for c in range(NC)]
                        tm = mm_group(bank(b_), pairs, tws + bank_dep(b_) + txq + tb1)
                        bank_tok[94] = tm
                        th = OP('vector', lambda h, p=bank(b_),
                                o=FM[:, bt_l * 4096 + jt * 512: bt_l * 4096 + (jt + 1) * 512],
                                sc=STATS[:, SC1 + bt_l:SC1 + bt_l + 1],
                                bb=PTS[:, ts(jt, 512)]:
                                h.scalar_tensor_tensor(o, p, sc, bb, ALU.mult, ALU.add), [tm])
                        set_bank(b_, th)
                        tprev.append(th)
                PTSB = PTS[:, :].bitcast(BF16)
                hq = [XQT[:, 0:4096], PTSB[:, 0:4096], PTSB[:, 4096:8192], QBF[:, 0:4096]]
                ttr_all = []
                for bt_l in range(4):
                    hsl = FM[:, ts(bt_l, 4096)]
                    tg = OP('scalar', lambda h, a=hsl: h.activation(a, a, AF.Gelu),
                            tprev + bank_dep(89))
                    trs = ln_stats(hsl, STATS[:, BSCR:BSCR + 1], STATS[:, 228:229],
                                   INNER, 1e-6, True, S1[:, 0:4096], [tg])
                    tq, _ = quant_act(hsl, QBF[:, 0:4096], STATS[:, INV2 + bt_l:INV2 + bt_l + 1],
                                      INNER, hsl, S1[:, 0:4096],
                                      [tg, trs], rm_ap=STATS[:, 228:229])
                    OP('vector', lambda h, o=STATS[:, SC2 + bt_l:SC2 + bt_l + 1],
                       a=STATS[:, INV2 + bt_l:INV2 + bt_l + 1], b_=STATS[:, 297:298]:
                       h.tensor_mul(o, a, b_), [tq])
                    dst = hq[bt_l]
                    ttr = [trans128(QBF[:, ts(c32, 128)], dst[:, ts(c32, 128)],
                                    (c32 & 1), deps=[tq] + bank_dep(93), bf=True)
                           for c32 in range(32)]
                    bank_tok[89] = ttr[-1]
                    bank_tok[93] = ttr[-1]
                    bank_tok[68] = tq
                    bank_tok[69] = tq
                    ttr_all.append(ttr[-1])
                tm_by_bank = {}
                for rnd in range(4):
                    tws = [LD(WSB[:, ts(i, 512)],
                              bass.AP(w2q_t, (rnd * 8 + (i % 8)) * 128 * D + (i // 8) * 512,
                                      [[D, 128], [1, 512]]),
                              bank_dep(94), ch=8) for i in range(16)]
                    for bt_l in range(4):
                        for Dt in range(2):
                            b_ = bt_l * 2 + Dt
                            pairs = [(hq[bt_l][:, (rnd * 8 + c) * 128:(rnd * 8 + c + 1) * 128],
                                      WSB[:, ts(Dt * 8 + c, 512)]) for c in range(NC)]
                            tm = mm_group(bank(b_), pairs,
                                          tws + ttr_all + (bank_dep(b_) if rnd == 0 else []),
                                          start=(rnd == 0), stop=(rnd == 3))
                            tm_by_bank[b_] = tm
                            bank_tok[94] = tm
                            bank_tok[89] = tm
                            bank_tok[97] = tm
                            bank_tok[93] = tm
                            bank_tok[68] = tm
                            bank_tok[69] = tm
                for bt_l in range(4):
                    bt = g * 4 + bt_l
                    for Dt in range(2):
                        b_ = bt_l * 2 + Dt
                        tep = OP('vector', lambda h, p=bank(b_), o=QH[:, 0:512],
                                 sc=STATS[:, SC2 + bt_l:SC2 + bt_l + 1],
                                 bb=QH[:, 1024 + Dt * 512:1024 + (Dt + 1) * 512]:
                                 h.scalar_tensor_tensor(o, p, sc, bb, ALU.mult, ALU.add),
                                 [tm_by_bank[b_]] + tb2 + bank_dep(96))
                        tsl = OP('scalar', lambda h, i_=QH[:, 0:512], o=QH[:, 512:1024]:
                                 h.activation(o, i_, AF.Silu), [tep])
                        set_bank(b_, tsl)
                        tst = ST(bass.AP(g0_t, bt * 128 * D + Dt * 512, [[D, 128], [1, 512]]),
                                 QH[:, 512:1024], [tsl])
                        bank_tok[96] = tst
                        st_g0.append(tst)
            return st_g0

        l0ffn()
        P.barrier()

        tt1 = LD(STATS[0:1, 179:180], bass.AP(kv0_out_t, 2 * T * D, [[1, 1], [1, 1]]), [t_cc1])
        tt2 = LD(STATS[0:1, 180:181], bass.AP(kv0_out_t, KV_SHARD + 2 * T * D, [[1, 1], [1, 1]]), [t_cc1])
        tmx = OP('vector', lambda h: h.reduce_max(STATS[0:1, 181:182], STATS[0:1, 179:181],
                                                  axis=AX.X), [tt1, tt2])
        tB0 = mk_bound(170, 181, BV_L0)

        st_x1 = {}

        def epi_l0(bt, Dt, pa, dd):
            sl = 2048 + (Dt & 1) * 512
            tg = LD(S1[:, sl:sl + 512], bass.AP(g0_t, bt * 128 * D + Dt * 512, [[D, 128], [1, 512]]),
                    bank_dep(88 + Dt), ch=10 + Dt)
            tm1 = OP('vector', lambda h, p=pa, g_=S1[:, sl:sl + 512], o=S1[:, 3072 + Dt * 512:3072 + (Dt + 1) * 512]:
                     h.scalar_tensor_tensor(o, p, 1.0, g_, ALU.bypass, ALU.mult),
                     dd + [tg] + bank_dep(78 + Dt))
            bank_tok[88 + Dt] = tm1
            xs = X[:, bt * 1024 + Dt * 512: bt * 1024 + (Dt + 1) * 512]
            tm2 = OP('vector', lambda h, o=xs, a=S1[:, 3072 + Dt * 512:3072 + (Dt + 1) * 512]:
                     h.tensor_add(o, o, a), [tm1])
            bank_tok[78 + Dt] = tm2
            if Dt == 1:
                st_x1[bt] = ST(x1_t[ts(bt, 128), :], X[:, ts(bt, 1024)], [tm2])
            return tm2

        attention('l0', kv0_out_t, KV_SHARD, qt0_t, wo0, bo0,
                  lambda kb: STATS[:, BV_L0:BV_L0 + 1], epi_l0, [t_cc1], bdeps=[tB0])
        P.barrier()

        for bt in range(NBT):
            xs = X[:, ts(bt, 1024)]
            tst = ln_stats(xs, STATS[:, MU1 + 2 * bt:MU1 + 2 * bt + 1],
                           STATS[:, MU1 + 2 * bt + 1:MU1 + 2 * bt + 2],
                           D, 1e-5, False, S1[:, 2048:3072], [st_x1[bt]])
            OP('vector', lambda h, s=xs,
               mu=STATS[:, MU1 + 2 * bt:MU1 + 2 * bt + 1],
               r=STATS[:, MU1 + 2 * bt + 1:MU1 + 2 * bt + 2]:
               h.tensor_scalar(s, s, mu, r, ALU.subtract, ALU.mult), [tst])
        P.barrier()

        LN1G, LN1B, LNXG, LNXB, LN2G, LN2B = 328, 336, 352, 360, 368, 376
        tln1 = load_cols(ln1g, 0, 8, LN1G)
        tln1 += load_cols(ln1b, 0, 8, LN1B)
        EIGHTH = 78
        OP('vector', lambda h: h.memset(STATS[:, EIGHTH:EIGHTH + 8], 0.125))

        def ln_apply_to_fm(gcol, bcol, gb_deps):
            for bt in range(NBT):
                slot = bt & 1
                xs = X[:, ts(bt, 1024)]
                xn = S1[:, ts(slot, 1024)]
                tst = ln_stats(xs, STATS[:, BSCR:BSCR + 1],
                               STATS[:, BSCR + 1:BSCR + 2],
                               D, 1e-5, False, S1[:, 2048:3072], bank_dep(86 + slot))
                txn = OP('vector', lambda h, s=xs, o=xn,
                         mu=STATS[:, BSCR:BSCR + 1], r=STATS[:, BSCR + 1:BSCR + 2]:
                         h.tensor_scalar(o, s, mu, r, ALU.subtract, ALU.mult), [tst])
                tevs = []
                for c in range(NC):
                    tevs.append(trans128(xn[:, ts(c, 128)],
                                         FM[:, c * 2048 + bt * 128:c * 2048 + (bt + 1) * 128],
                                         (c & 1), deps=[txn] + list(gb_deps),
                                         scale=STATS[:, gcol + c:gcol + c + 1],
                                         bias_ap=STATS[:, bcol + c:bcol + c + 1]))
                bank_tok[86 + slot] = tevs[-1]

        ln_apply_to_fm(LN1G, LN1B, tln1)
        P.barrier()

        kv1_done = []
        tcb1 = load_cols(cab, 0, 8, 384, scale=0.125)
        tq1 = proj_fm(caw, 0, D, (0, 16384), mk_qt_writer(qt1_t),
                      scale_col=EIGHTH, bias_col0=384, ncol=164, nscr=S1[:, 2048:2560],
                      aux=tcb1 + [P.now('vector')])
        norm_fin(164, 172, [P.now('vector')])
        tcb2 = load_cols(cab, 8, 8, 392)
        tk1 = proj_fm(caw, D, D, (0, 16384), mk_kv_writer(kv1_in_t, kv1_done),
                      bias_col0=392, ncol=163, nscr=S1[:, 2048:2560], deps=tq1, aux=tcb2)
        tnf1 = norm_fin(163, 171, [P.now('vector')])
        tnc1 = OP('vector', lambda h: h.tensor_copy(STATS[0:1, 122:123],
                                                    STATS[0:1, 171:172]), [tnf1, tpad])
        kv1_done.append(ST(bass.AP(kv1_in_t, 2 * T * D, [[1, 32], [1, 1]]),
                           STATS[0:1, 122:154], [tnc1]))
        cabv = bass.AP(cab.tensor, 16 * 128, [[1024, 1], [1, 1024]])
        proj_tm(caw, 2 * D, D, lambda c, btl: FM[:, c * 2048 + btl * 128: c * 2048 + (btl + 1) * 128],
                NBT, mk_vwriter(kv1_in_t, kv1_done), bias_row=cabv, deps=tk1)
        t_cc2 = P.cc(lambda h: h.collective_compute(
            "AllGather", ALU.bypass, replica_groups=[[0, 1], [2, 3], [4, 5], [6, 7]],
            ins=[kv1_in_t.ap().opt()], outs=[kv1_out_t.ap().opt()]), kv1_done)
        P.barrier()

        SCU = 260

        def l1ffn():
            st_g1 = []
            for g in range(4):
                txq = []
                for bt_l in range(4):
                    bt = g * 4 + bt_l
                    par = bt_l & 1
                    tq, _ = quant_act(X[:, ts(bt, 1024)], QBF[:, ts(bt_l, 1024)],
                                      STATS[:, INV1 + bt_l:INV1 + bt_l + 1], D,
                                      PTS[:, ts(par, 1024)],
                                      PTS[:, 2048 + par * 1024:3072 + par * 1024],
                                      bank_dep(68 + par), tset=par)
                    bank_tok[68 + par] = tq
                    for c in range(NC):
                        t = trans128(QBF[:, bt_l * 1024 + c * 128: bt_l * 1024 + (c + 1) * 128],
                                     XQT[:, bt_l * 1024 + c * 128: bt_l * 1024 + (c + 1) * 128],
                                     (c & 1), deps=[tq] + bank_dep(93), bf=True)
                    txq.append(t)
                    OP('vector', lambda h, o=STATS[:, SC1 + bt_l:SC1 + bt_l + 1],
                       a=STATS[:, INV1 + bt_l:INV1 + bt_l + 1], b_=STATS[:, 305:306]:
                       h.tensor_mul(o, a, b_), [tq])
                    OP('vector', lambda h, o=STATS[:, SCU + bt_l:SCU + bt_l + 1],
                       a=STATS[:, INV1 + bt_l:INV1 + bt_l + 1], b_=STATS[:, 313:314]:
                       h.tensor_mul(o, a, b_), [tq])
                for jt in range(8):
                    twg = [LD(WSB[:, ts(c, 512)],
                              bass.AP(wgq_t, c * 128 * INNER + jt * 512, [[INNER, 128], [1, 512]]),
                              bank_dep(94), ch=8) for c in range(NC)]
                    twu = [LD(WSB[:, 4096 + c * 512: 4096 + (c + 1) * 512],
                              bass.AP(wuq_t, c * 128 * INNER + jt * 512, [[INNER, 128], [1, 512]]),
                              bank_dep(94), ch=8) for c in range(NC)]
                    for bt_l in range(4):
                        pg = [(XQT[:, bt_l * 1024 + c * 128: bt_l * 1024 + (c + 1) * 128],
                               WSB[:, ts(c, 512)]) for c in range(NC)]
                        pu = [(XQT[:, bt_l * 1024 + c * 128: bt_l * 1024 + (c + 1) * 128],
                               WSB[:, 4096 + c * 512: 4096 + (c + 1) * 512]) for c in range(NC)]
                        tmg = mm_group(bank(4), pg, twg + bank_dep(4) + txq)
                        bank_tok[94] = tmg
                        tmu = mm_group(bank(5), pu, twu + bank_dep(5) + txq)
                        bank_tok[94] = tmu
                        tsl = OP('scalar', lambda h, p=bank(4), o=QH[:, 0:512],
                                 sc=STATS[:, SC1 + bt_l:SC1 + bt_l + 1]:
                                 h.activation(o, p, AF.Silu, scale=sc),
                                 [tmg] + bank_dep(96))
                        set_bank(4, tsl)
                        tup = OP('vector', lambda h, p=bank(5), o=QH[:, 512:1024],
                                 sc=STATS[:, SCU + bt_l:SCU + bt_l + 1]:
                                 h.tensor_scalar(o, p, sc, None, ALU.mult),
                                 [tmu] + bank_dep(96))
                        set_bank(5, tup)
                        tgu = OP('vector', lambda h,
                                 o=FM[:, bt_l * 4096 + jt * 512: bt_l * 4096 + (jt + 1) * 512]:
                                 h.tensor_mul(o, QH[:, 0:512], QH[:, 512:1024]), [tsl, tup])
                        bank_tok[96] = tgu
                PTSB = PTS[:, :].bitcast(BF16)
                hq = [XQT[:, 0:4096], PTSB[:, 0:4096], PTSB[:, 4096:8192], QBF[:, 0:4096]]
                ttr_all = []
                for bt_l in range(4):
                    gsl = FM[:, ts(bt_l, 4096)]
                    tq, _ = quant_act(gsl, QBF[:, 0:4096], STATS[:, INV2 + bt_l:INV2 + bt_l + 1],
                                      INNER, gsl, S1[:, 0:4096],
                                      bank_dep(96) + bank_dep(89))
                    OP('vector', lambda h, o=STATS[:, SC2 + bt_l:SC2 + bt_l + 1],
                       a=STATS[:, INV2 + bt_l:INV2 + bt_l + 1], b_=STATS[:, 321:322]:
                       h.tensor_mul(o, a, b_), [tq])
                    dst = hq[bt_l]
                    ttr = [trans128(QBF[:, ts(c32, 128)], dst[:, ts(c32, 128)],
                                    (c32 & 1), deps=[tq] + bank_dep(93), bf=True)
                           for c32 in range(32)]
                    bank_tok[89] = ttr[-1]
                    bank_tok[93] = ttr[-1]
                    bank_tok[68] = tq
                    bank_tok[69] = tq
                    ttr_all.append(ttr[-1])
                tm_by_bank = {}
                for rnd in range(4):
                    tws = [LD(WSB[:, ts(i, 512)],
                              bass.AP(wdq_t, (rnd * 8 + (i % 8)) * 128 * D + (i // 8) * 512,
                                      [[D, 128], [1, 512]]),
                              bank_dep(94), ch=8) for i in range(16)]
                    for bt_l in range(4):
                        for Dt in range(2):
                            b_ = bt_l * 2 + Dt
                            pairs = [(hq[bt_l][:, (rnd * 8 + c) * 128:(rnd * 8 + c + 1) * 128],
                                      WSB[:, ts(Dt * 8 + c, 512)]) for c in range(NC)]
                            tm = mm_group(bank(b_), pairs,
                                          tws + ttr_all + (bank_dep(b_) if rnd == 0 else []),
                                          start=(rnd == 0), stop=(rnd == 3))
                            tm_by_bank[b_] = tm
                            bank_tok[94] = tm
                            bank_tok[89] = tm
                            bank_tok[93] = tm
                            bank_tok[96] = tm
                            bank_tok[68] = tm
                            bank_tok[69] = tm
                for bt_l in range(4):
                    bt = g * 4 + bt_l
                    for Dt in range(2):
                        b_ = bt_l * 2 + Dt
                        tsl = OP('scalar', lambda h, p=bank(b_), o=QH[:, 0:512],
                                 sc=STATS[:, SC2 + bt_l:SC2 + bt_l + 1]:
                                 h.activation(o, p, AF.Silu, scale=sc),
                                 [tm_by_bank[b_]] + bank_dep(95))
                        set_bank(b_, tsl)
                        tst = ST(bass.AP(g1_t, bt * 128 * D + Dt * 512, [[D, 128], [1, 512]]),
                                 QH[:, 0:512], [tsl])
                        bank_tok[95] = tst
                        bank_tok[96] = tst
                        st_g1.append(tst)
            return st_g1

        l1ffn()
        P.barrier()

        tt1 = LD(STATS[0:1, 179:180], bass.AP(kv1_out_t, 2 * T * D, [[1, 1], [1, 1]]), [t_cc2])
        tt2 = LD(STATS[0:1, 180:181], bass.AP(kv1_out_t, KV_SHARD + 2 * T * D, [[1, 1], [1, 1]]), [t_cc2])
        OP('vector', lambda h: h.reduce_max(STATS[0:1, 181:182], STATS[0:1, 179:181],
                                            axis=AX.X), [tt1, tt2])
        tB1 = mk_bound(172, 181, BV_SELF)

        def epi_add(bt, Dt, pa, dd):
            xs = X[:, bt * 1024 + Dt * 512: bt * 1024 + (Dt + 1) * 512]
            return OP('vector', lambda h, p=pa, o=xs:
                      h.scalar_tensor_tensor(o, p, 1.0, o, ALU.bypass, ALU.add), dd)

        attention('self', kv1_out_t, KV_SHARD, qt1_t, apw, apb,
                  lambda kb: STATS[:, BV_SELF:BV_SELF + 1], epi_add, [t_cc2], bdeps=[tB1])
        P.barrier()

        tlnx = load_cols(lnxg, 0, 8, LNXG)
        tlnx += load_cols(lnxb, 0, 8, LNXB)
        tqab = load_cols(qab, 0, 8, 86, scale=0.125)
        ln_apply_to_fm(LNXG, LNXB, tlnx)
        P.barrier()
        proj_fm(qaw, 0, D, (0, 16384), mk_qt_writer(qtx_t),
                scale_col=EIGHTH, bias_col0=86, ncol=165, nscr=S1[:, 2048:2560],
                aux=tqab + [P.now('vector')])
        norm_fin(165, 173, [P.now('vector')])
        P.barrier()

        ttx = [LD(STATS[0:1, 184 + r:185 + r],
                  bass.AP(kvx_out_t, r * KVX_SHARD + 2 * 256 * D, [[1, 1], [1, 1]]), [t_cc3])
               for r in range(8)]
        OP('vector', lambda h: h.reduce_max(STATS[0:1, 168:169], STATS[0:1, 184:192],
                                            axis=AX.X), ttx)
        tsr = LD(QH[0:1, 0:2048], scorer_row, ch=11)
        tsrd = OP('vector', lambda h: h.reduce_max(STATS[0:1, 174:175], QH[0:1, 0:2048],
                                                   axis=AX.X), [tsr])
        tBx = mk_bound(173, 168, BV_CROSS, extra_col=174)
        tcv = []
        for kb in range(16):
            tcv.append(OP('vector', lambda h, o=STATS[:, CBV + kb:CBV + kb + 1],
                          s=STATS[:, 100 + kb:101 + kb], b_=STATS[:, BV_CROSS:BV_CROSS + 1]:
                          h.tensor_scalar(o, s, b_, None, ALU.add), [t_sc, tBx]))
        attention('cross', kvx_out_t, KVX_SHARD, qtx_t, xpw, xpb,
                  lambda kb: STATS[:, CBV + kb:CBV + kb + 1], epi_add, [t_cc3, tsrd],
                  bdeps=[tcv[-1]])
        P.barrier()

        tln2 = load_cols(ln2g, 0, 8, LN2G)
        tln2 += load_cols(ln2b, 0, 8, LN2B)
        ln_apply_to_fm(LN2G, LN2B, tln2)
        P.barrier()

        tfcb = load_cols(fcb, 0, 32, 96)

        def fc_writer(hc, tt_, pa, dd):
            o = QH[:, ts(tt_ & 1, 512)]
            tg = OP('scalar', lambda h, p=pa, o_=o:
                    h.activation(o_, p, AF.Gelu_apprx_tanh), dd + bank_dep(87 + (tt_ & 1)))
            tst = ST(bass.AP(hsp_t, hc * 128 * T + tt_ * 512, [[T, 128], [1, 512]]),
                     o, [tg])
            bank_tok[87 + (tt_ & 1)] = tst
            return tst

        proj_fm(fcw, 0, INNER, (0, 16384), fc_writer, bias_col0=96, aux=tfcb)
        P.barrier()

        tmb = LD(QH[0:1, 1024:2048], mpb, ch=9)
        acc_map = {}
        for Dt in range(2):
            for rnd in range(4):
                tws = [LD(WS[:, ts(c, 512)],
                          mpw[ts(rnd * 8 + c, 128), Dt * 512:(Dt + 1) * 512],
                          bank_dep(94), ch=8) for c in range(NC)]
                for bt in range(NBT):
                    sl = ts(bt & 1, 1024)
                    tl = None
                    for c_ in range(NC):
                        tl = LD(S1[:, (bt & 1) * 1024 + c_ * 128:(bt & 1) * 1024 + (c_ + 1) * 128],
                                bass.AP(hsp_t, (rnd * 8 + c_) * 128 * T + bt * 128,
                                        [[T, 128], [1, 128]]),
                                bank_dep(74 + (bt & 1)), ch=10 + (bt & 1))
                    b_ = 4 + (bt & 1)
                    pairs = [(S1[:, (bt & 1) * 1024 + c * 128:(bt & 1) * 1024 + (c + 1) * 128],
                              WS[:, ts(c, 512)]) for c in range(NC)]
                    last_rnd = (rnd == 3)
                    tm = mm_group(bank(b_), pairs, tws + [tl] + bank_dep(b_),
                                  start=True, stop=not last_rnd)
                    bank_tok[74 + (bt & 1)] = tm
                    bank_tok[94] = tm
                    if last_rnd:
                        tm = bias_mm(bank(b_), QH[0:1, 1024 + Dt * 512:1024 + (Dt + 1) * 512],
                                     [tm, tmb])
                    fs = FM[:, bt * 1024 + Dt * 512: bt * 1024 + (Dt + 1) * 512]
                    prev = acc_map.get((bt, Dt))
                    pdep = [prev] if prev else bank_dep(73)
                    if rnd == 0:
                        ta = OP('vector', lambda h, p=bank(b_), o=fs:
                                h.tensor_copy(o, p), [tm] + pdep)
                    else:
                        ta = OP('vector', lambda h, p=bank(b_), o=fs:
                                h.tensor_tensor(o, p, o, ALU.add), [tm] + pdep)
                    acc_map[(bt, Dt)] = ta
                    set_bank(b_, ta)
        for bt in range(NBT):
            for Dt in range(2):
                xs = X[:, bt * 1024 + Dt * 512: bt * 1024 + (Dt + 1) * 512]
                ta2 = OP('vector', lambda h, o=xs, a=FM[:, bt * 1024 + Dt * 512: bt * 1024 + (Dt + 1) * 512]:
                         h.tensor_add(o, o, a), [P.now('vector')])
                bank_tok[73] = ta2
        P.barrier()

        last_st = None
        for bt in range(NBT):
            tg = LD(S1[:, 0:1024], g1_t.ap()[ts(bt, 128), :], bank_dep(83), ch=10)
            tx = LD(S1[:, 1024:2048], x1_t.ap()[ts(bt, 128), :], bank_dep(83), ch=11)
            tm1 = OP('vector', lambda h, a=X[:, ts(bt, 1024)], g_=S1[:, 0:1024],
                     o=S1[:, 2048:3072]: h.tensor_mul(o, a, g_), [tg])
            tm2 = OP('vector', lambda h, a=S1[:, 2048:3072], b_=S1[:, 1024:2048],
                     o=S1[:, 3072:4096]: h.tensor_add(o, a, b_),
                     [tm1, tx] + bank_dep(76))
            bank_tok[83] = tm2
            last_st = ST(y[ts(bt, 128), :], S1[:, 3072:4096], [tm2])
            bank_tok[76] = last_st

        if DBG is not None:
            P.barrier()
            srcmap = {'qt0': qt0_t, 'qt1': qt1_t, 'qtx': qtx_t, 'g0': g0_t,
                      'g1': g1_t, 'x1sp': x1_t, 'hsp': hsp_t, 'kv0_in': kv0_in_t,
                      'kv0_out': kv0_out_t, 'kv1_in': kv1_in_t, 'kvx_in': kvx_in_t,
                      'kvx_out': kvx_out_t, 'w1qT': w1q_t, 'w2qT': w2q_t, 'xnfd': xnfd_t, 'xnfd1': xnfd_t, 'qt0e': xnfd_t}
            st_ = srcmap[DBG[0]]
            n_ = DBG[1]
            rows_ = n_ // 2048
            if rows_:
                ST(bass.AP(dbg_out, 0, [[2048, rows_], [1, 2048]]),
                   bass.AP(st_, 0, [[2048, rows_], [1, 2048]]))
            rem_ = n_ - rows_ * 2048
            if rem_:
                ST(bass.AP(dbg_out, rows_ * 2048, [[1, rem_]]),
                   bass.AP(st_, rows_ * 2048, [[1, rem_]]))

        P.wait('gpsimd', [(k, v) for k, v in P.cnt.items()
                          if k.startswith('d:gpsimd') and v])

        with nc.Block() as block:
            P.emit(block, sems)
    return nc, wshapes


# version 08 (final)
# speedup vs baseline: 1.1931x; 1.1931x over previous
import numpy as np
from contextlib import ExitStack
import concourse.bass as bass
import concourse.mybir as mybir
from concourse.bass_utils import run_bass_kernel_spmd

F32 = mybir.dt.float32
BF16 = mybir.dt.bfloat16
I32 = mybir.dt.int32
AF = mybir.ActivationFunctionType
ALU = mybir.AluOpType
AX = mybir.AxisListType

NCORES = 8
D = 1024
NH = 16
DH = 64
S = 4096
T = 2048
NBT = 16
NC = 8
INNER = 4096
ENC = 2048
NEG = np.float32(np.finfo(np.float32).min)

ENGINES = ('sync', 'tensor', 'vector', 'scalar', 'gpsimd')


class Prog:

    def __init__(self, nc):
        self.nc = nc
        self.steps = {e: [] for e in ENGINES}
        self.cnt = {('p:' + e): 0 for e in ENGINES}
        self.cnt['cc'] = 0

    def _fix(self, deps):
        out = []
        for k, v in deps:
            if k.startswith('d:'):
                out.append((k, self.cnt.get(k, v)))
            else:
                out.append((k, v))
        return tuple(out)

    def do(self, eng, fn, deps=()):
        self.cnt['p:' + eng] += 1
        self.steps[eng].append(('op', self._fix(deps), fn, 'p:' + eng))
        return ('p:' + eng, self.cnt['p:' + eng])

    def dma(self, eng, fn, deps=(), n=1, ch=0):
        key = 'd:%s:%d' % (eng, ch)
        self.cnt[key] = self.cnt.get(key, 0) + 16 * n
        self.steps[eng].append(('dma', self._fix(deps), fn, key))
        return (key, self.cnt[key])

    def cc(self, fn, deps=()):
        self.cnt['cc'] += 1
        self.steps['gpsimd'].append(('op', self._fix(deps), fn, 'cc'))
        return ('cc', self.cnt['cc'])

    def wait(self, eng, deps):
        self.steps[eng].append(('wait', self._fix(deps), None, None))

    def now(self, eng):
        return ('p:' + eng, self.cnt['p:' + eng])

    def barrier(self):
        toks = [(k, v) for k, v in self.cnt.items()
                if v and k != 'cc']
        for e in ENGINES:
            self.wait(e, toks)

    def emit(self, block, sems):
        for eng in ENGINES:
            self._emit_engine(block, eng, sems)

    def _emit_engine(self, block, eng, sems):
        steps = self.steps[eng]
        if not steps:
            return
        dec = getattr(block, eng)
        last_wait = {}
        prog = self
        own = 'p:' + eng

        def body(h):
            counts = {k: 0 for k in prog.cnt}
            ecount = 0
            for kind, deps, fn, inckey in steps:
                for sem_key, val in deps:
                    if sem_key == own:
                        if val > ecount - 8 and val > last_wait.get(own, -1):
                            h.wait_ge(sems[own], val)
                            last_wait[own] = val
                        continue
                    if last_wait.get(sem_key, -1) >= val:
                        continue
                    h.wait_ge(sems[sem_key], val)
                    last_wait[sem_key] = val
                if inckey == own:
                    ecount += 1
                if kind == 'wait':
                    continue
                if kind == 'dma':
                    instrs = fn(h)
                    if not isinstance(instrs, (list, tuple)):
                        instrs = [instrs]
                    for i in instrs:
                        i.then_inc(sems[inckey], 16)
                    counts[inckey] += 16 * len(instrs)
                else:
                    instr = fn(h)
                    assert instr is not None, f"step on {eng} returned None"
                    instr.then_inc(sems[inckey], 1)
                    counts[inckey] += 1

        dec(body)


def ts(i, n):
    return slice(i * n, (i + 1) * n)



def _stripe_rows(par):
    idx = []
    for bt in range(NBT):
        j = 2 * bt + par
        idx.append(np.arange(j * 128, (j + 1) * 128))
    return np.concatenate(idx)


def _mask_self(par):
    m = np.zeros((32, 128, 128), dtype=np.float32)
    for kb in range(32):
        t0 = kb // 2
        qblk = 2 * t0 + par
        k_tok = kb * 128 + np.arange(128)[:, None]
        q_tok = qblk * 128 + np.arange(128)[None, :]
        m[kb] = np.where(q_tok >= k_tok, 0.0, NEG)
    return m


def _mask_cross(par):
    m = np.zeros((16, 2, 128, 128), dtype=np.float32)
    for kb in range(16):
        t0 = max(0, (kb - 1) // 2)
        for j in range(2):
            t = t0 + j
            if t > 15:
                m[kb, j] = 0.0
                continue
            qblk = 2 * t + par
            k_tok = kb * 128 + np.arange(128)[:, None]
            q_tok = qblk * 128 + np.arange(128)[None, :]
            m[kb, j] = np.where(k_tok <= q_tok + 2, 0.0, NEG)
    return m


def kernel(**inp):
    full_x = np.ascontiguousarray(inp['L0_attn_x'])
    enc = np.ascontiguousarray(inp['L1_attn_encoder_hidden_states'][0])
    scorer = np.ascontiguousarray(inp['L1_attn_scorer_logits'][0])

    weight_names = [k for k in inp.keys()
                    if k not in ('L0_attn_x', 'L1_attn_encoder_hidden_states',
                                 'L1_attn_scorer_logits')]

    nc, wshapes = build_program()

    in_maps = []
    for cid in range(NCORES):
        b = cid // 2
        par = cid & 1
        rows = _stripe_rows(par)
        m = {
            'x0': np.ascontiguousarray(full_x[b][rows]),
            'enc_slice': np.ascontiguousarray(enc[cid * 256:(cid + 1) * 256]),
            'scorer_cols': np.ascontiguousarray(scorer.reshape(16, 128).T),
            'scorer_row': np.ascontiguousarray(scorer.reshape(1, 2048)),
            'ident': np.eye(128, dtype=np.float32),
            'mask_self': _mask_self(par),
            'mask_cross': _mask_cross(par),
        }
        for k in weight_names:
            m[k] = np.ascontiguousarray(inp[k].reshape(wshapes[k]))
        in_maps.append(m)

    res = run_bass_kernel_spmd(nc, in_maps, core_ids=list(range(NCORES)))

    out = np.empty((4, 4096, 1024), dtype=np.float32)
    for cid in range(NCORES):
        b = cid // 2
        par = cid & 1
        rows = _stripe_rows(par)
        out[b][rows] = res.results[cid]['y']
    return out



DBG = None


def build_program():
    nc = bass.Bass("TRN2", target_bir_lowering=False)
    wshapes = {}

    def win(name, shape):
        wshapes[name] = tuple(shape)
        return nc.dram_tensor(name, list(shape), F32, kind="ExternalInput").ap()

    x0 = nc.dram_tensor('x0', [T, D], F32, kind="ExternalInput").ap()
    enc_slice = nc.dram_tensor('enc_slice', [256, D], F32, kind="ExternalInput").ap()
    scorer_cols = nc.dram_tensor('scorer_cols', [128, 16], F32, kind="ExternalInput").ap()
    scorer_row = nc.dram_tensor('scorer_row', [1, ENC], F32, kind="ExternalInput").ap()
    ident_in = nc.dram_tensor('ident', [128, 128], F32, kind="ExternalInput").ap()
    mask_self_d = nc.dram_tensor('mask_self', [32, 128, 128], F32, kind="ExternalInput").ap()
    mask_cross_d = nc.dram_tensor('mask_cross', [16, 2, 128, 128], F32, kind="ExternalInput").ap()

    ns0 = win('L0_attn_norm_scale', (8, 128))
    wq0 = win('L0_attn_wq', (D, D))
    wk0 = win('L0_attn_wk', (D, D))
    wv0 = win('L0_attn_wv', (D, D))
    wo0 = win('L0_attn_w_out', (D, D))
    bo0 = win('L0_attn_b_out', (1, D))
    w1 = win('L0_ffn_w1', (INNER, D))
    b1 = win('L0_ffn_b1', (1, INNER))
    w2 = win('L0_ffn_w2', (D, INNER))
    b2 = win('L0_ffn_b2', (1, D))
    ln1g = win('L1_attn_ln1_g', (8, 128))
    ln1b = win('L1_attn_ln1_b', (8, 128))
    caw = win('L1_attn_c_attn_w', (D, 3 * D))
    cab = win('L1_attn_c_attn_b', (24, 128))
    apw = win('L1_attn_attn_proj_w', (D, D))
    apb = win('L1_attn_attn_proj_b', (1, D))
    lnxg = win('L1_attn_lnx_g', (8, 128))
    lnxb = win('L1_attn_lnx_b', (8, 128))
    qaw = win('L1_attn_q_attn_w', (D, D))
    qab = win('L1_attn_q_attn_b', (8, 128))
    xkw = win('L1_attn_x_kv_w', (D, 2 * D))
    xkb = win('L1_attn_x_kv_b', (16, 128))
    xpw = win('L1_attn_x_proj_w', (D, D))
    xpb = win('L1_attn_x_proj_b', (1, D))
    ln2g = win('L1_attn_ln2_g', (8, 128))
    ln2b = win('L1_attn_ln2_b', (8, 128))
    fcw = win('L1_attn_fc_w', (D, INNER))
    fcb = win('L1_attn_fc_b', (32, 128))
    mpw = win('L1_attn_mlp_proj_w', (INNER, D))
    mpb = win('L1_attn_mlp_proj_b', (1, D))
    wg = win('L1_ffn_w_gate', (INNER, D))
    wu = win('L1_ffn_w_up', (INNER, D))
    wd = win('L1_ffn_w_down', (D, INNER))

    y = nc.dram_tensor('y', [T, D], F32, kind="ExternalOutput").ap()
    dbg_out = None
    if DBG is not None:
        dbg_out = nc.dram_tensor('dbgout', [DBG[1]], F32, kind="ExternalOutput")

    KV_SHARD = 2 * T * D + 32
    kv0_in_t = nc.dram_tensor('kv0_in', [KV_SHARD], F32)
    kv0_out_t = nc.dram_tensor('kv0_out', [2 * KV_SHARD], F32)
    kv1_in_t = nc.dram_tensor('kv1_in', [KV_SHARD], F32)
    kv1_out_t = nc.dram_tensor('kv1_out', [2 * KV_SHARD], F32)
    KVX_SHARD = 2 * 256 * D + 16
    kvx_in_t = nc.dram_tensor('kvx_in', [KVX_SHARD], F32)
    kvx_out_t = nc.dram_tensor('kvx_out', [8 * KVX_SHARD], F32)
    qt0_t = nc.dram_tensor('qt0', [D, T], F32)
    qt1_t = nc.dram_tensor('qt1', [D, T], F32)
    qtx_t = nc.dram_tensor('qtx', [D, T], F32)
    g0_t = nc.dram_tensor('g0', [T, D], F32)
    g1_t = nc.dram_tensor('g1', [T, D], F32)
    x1_t = nc.dram_tensor('x1sp', [T, D], F32)
    hsp_t = nc.dram_tensor('hsp', [INNER, T], F32)
    xnfd_t = nc.dram_tensor('xnfd', [128 * 16384], F32)
    w1q_t = nc.dram_tensor('w1qT', [D, INNER], BF16)
    w2q_t = nc.dram_tensor('w2qT', [INNER, D], BF16)
    wgq_t = nc.dram_tensor('wgqT', [D, INNER], BF16)
    wuq_t = nc.dram_tensor('wuqT', [D, INNER], BF16)
    wdq_t = nc.dram_tensor('wdqT', [INNER, D], BF16)
    dbg_t = nc.dram_tensor('dbg', [128, 16384], F32, kind="ExternalOutput").ap() if DBG else None

    P = Prog(nc)

    with ExitStack() as st:
        ec = st.enter_context
        X = ec(nc.sbuf_tensor("X", [128, 16384], F32))
        FM = ec(nc.sbuf_tensor("FM", [128, 16384], F32))
        S1 = ec(nc.sbuf_tensor("S1", [128, 4096], F32))
        PTS = ec(nc.sbuf_tensor("PTS", [128, 4096], F32))
        QH = ec(nc.sbuf_tensor("QH", [128, 2048], F32))
        KVB = ec(nc.sbuf_tensor("KVB", [128, 1288], F32))
        XQT = ec(nc.sbuf_tensor("XQT", [128, 4096], BF16))
        QBF = ec(nc.sbuf_tensor("QBF", [128, 4096], BF16))
        WS = ec(nc.sbuf_tensor("WS", [128, 4096], F32))
        STATS = ec(nc.sbuf_tensor("STATS", [128, 400], F32))
        ONE = ec(nc.sbuf_tensor("ONE", [128, 128], F32))
        IDT = ec(nc.sbuf_tensor("IDT", [128, 128], F32))
        IDB = ec(nc.sbuf_tensor("IDB", [128, 128], BF16))
        PS = ec(nc.psum_tensor("PS", [128, 4096], F32))

        sems = {}
        for e in ENGINES:
            sems['p:' + e] = ec(nc.semaphore('sp_' + e))
        for e in ('sync', 'gpsimd'):
            for chn in range(16):
                sems['d:%s:%d' % (e, chn)] = ec(nc.semaphore('sd_%s_%d' % (e, chn)))
        sems['cc'] = ec(nc.semaphore('scc'))

        def LD(dst, src, deps=(), eng='sync', ch=0):
            return P.dma(eng, lambda h, d=dst, s=src: h.dma_start(out=d, in_=s), deps, ch=ch)

        def ST(dst, src, deps=(), eng='gpsimd'):
            return P.dma(eng, lambda h, d=dst, s=src: h.dma_start(out=d, in_=s), deps)

        def OP(eng, fn, deps=()):
            return P.do(eng, fn, deps)

        bank_tok = {}
        def bank_dep(b):
            t = bank_tok.get(b)
            return [t] if t else []

        def set_bank(b, tok):
            bank_tok[b] = tok

        def bank(b, n=512):
            return PS[:, b * 512: b * 512 + n]

        def mm_group(out_aps, pairs, deps=(), start=True, stop=True, eng='tensor'):
            def f(h, out_aps=out_aps, pairs=pairs, start=start, stop=stop):
                last = None
                n = len(pairs)
                for i, (lt, rh) in enumerate(pairs):
                    o = out_aps[i] if isinstance(out_aps, list) else out_aps
                    last = h.matmul(o, lt, rh, start=(start and i == 0),
                                    stop=(stop and i == n - 1))
                return last
            return P.do('tensor', f, deps)

        def bias_mm(out_ap, bias_row, deps=(), start=False, stop=True, base=0):
            def f(h, o=out_ap, br=bias_row, base=base, start=start, stop=stop):
                return h.matmul(o, ONE[base:base + 1, 0:128], br,
                                start=start, stop=stop)
            return P.do('tensor', f, deps)

        ev_cycle = [0]
        def evac_eng():
            ev_cycle[0] ^= 1
            return 'vector' if ev_cycle[0] else 'scalar'

        def COPY(eng, dst, src, deps=()):
            if eng == 'scalar':
                return OP('scalar', lambda h, d=dst, s=src: h.activation(d, s, AF.Copy), deps)
            return OP(eng, lambda h, d=dst, s=src: h.tensor_copy(d, s), deps)

        def trans128(src, dst, b, deps=(), scale=None, bias_ap=None, bf=False):
            pst = PS[:, b * 512: b * 512 + 128]
            if bf:
                pst = pst.bitcast(BF16)[:, 0:128]
            t1 = OP('tensor', lambda h, s=src, p=pst: h.transpose(p, s, (IDB if bf else IDT)[:]),
                    list(deps) + bank_dep(b))
            e = evac_eng()
            if scale is not None:
                def f(h, d=dst, p=pst, sc=scale, bi=bias_ap):
                    if bi is not None:
                        return h.tensor_scalar(d, p, sc, bi, ALU.mult, ALU.add)
                    return h.tensor_scalar(d, p, sc, None, ALU.mult)
                t2 = OP('vector', f, [t1])
            else:
                t2 = COPY(e, dst, pst, [t1])
            set_bank(b, t2)
            return t2

        def prep_pass1(wd, R, C, scol, chA, chB, deps=()):
            acc = STATS[:, scol + 2:scol + 3]
            tacc = OP('vector', lambda h, o=acc: h.memset(o, 0.0), deps)
            cons = {}
            np_ = (R // 128) * (C // 1024)
            for p_ in range(np_):
                j, q = divmod(p_, C // 1024)
                sl = PTS[:, ts(p_ & 1, 1024)]
                tl = P.dma('sync', lambda h, d=sl, sr=wd[ts(j, 128), ts(q, 1024)]:
                           h.dma_start(out=d, in_=sr),
                           ([cons[p_ & 1]] if (p_ & 1) in cons else list(deps)),
                           ch=chA if (p_ & 1) == 0 else chB)
                tr2 = OP('vector', lambda h, o=STATS[:, scol + 4 + (p_ & 1):scol + 5 + (p_ & 1)],
                         s=sl: h.reduce_sum(o, s, axis=AX.X, apply_absolute_value=True), [tl])
                cons[p_ & 1] = tr2
                tacc = OP('vector', lambda h, o=acc, a=STATS[:, scol + 4 + (p_ & 1):scol + 5 + (p_ & 1)]:
                          h.tensor_add(o, o, a), [tr2, tacc])
            return [tacc, cons.get(0), cons.get(1)]

        OP('vector', lambda h: h.memset(ONE[:], 1.0))
        t_id = LD(IDT[:], ident_in)
        t_sc = LD(STATS[0:128, 100:116], scorer_cols)
        t_x = [LD(X[:, ts(bt, 1024)], x0[ts(bt, 128), :]) for bt in range(NBT)]
        OP('vector', lambda h: h.tensor_copy(IDB[:], IDT[:]), [t_x[-1]])
        p1_toks = {}
        p1_toks[288] = prep_pass1(w1, INNER, D, 288, 1, 2)
        p1_toks[296] = prep_pass1(w2, D, INNER, 296, 1, 2, deps=[p1_toks[288][0]])
        p1_toks[304] = prep_pass1(wg, INNER, D, 304, 1, 2, deps=[p1_toks[296][0]])
        p1_toks[312] = prep_pass1(wu, INNER, D, 312, 1, 2, deps=[p1_toks[304][0]])
        p1_toks[320] = prep_pass1(wd, D, INNER, 320, 1, 2, deps=[p1_toks[312][0]])

        MU0, MU1 = 0, 32
        BSCR = 192
        BV_L0, BV_SELF, BV_CROSS = 203, 204, 205
        CBV = 208
        QSCR = 240

        O2 = 76
        to2a = OP('vector', lambda h: h.memset(STATS[:, O2:O2 + 2], 0.0))
        to2b = OP('vector', lambda h: h.memset(STATS[0:64, O2:O2 + 1], 1.0), [to2a])
        OP('vector', lambda h: h.memset(STATS[64:128, O2 + 1:O2 + 2], 1.0), [to2b])

        stats_chain = {0: None, 1: None}

        def ln_stats(src, mu_ap, r_ap, w, eps, rms, scr, deps, tset=0):
            dd = list(deps) + ([stats_chain[tset]] if stats_chain[tset] else [])
            b0 = 64 + 6 * tset
            sq = STATS[:, b0:b0 + 1]
            sm = STATS[:, b0 + 1:b0 + 2]
            m2 = STATS[:, b0 + 2:b0 + 3]
            vr = STATS[:, b0 + 3:b0 + 4]
            sd = STATS[:, b0 + 4:b0 + 5]
            t1 = OP('scalar', lambda h, s=src, o=scr, a=sq:
                    h.activation(o, s, AF.Square, accum_out=a), dd)
            if rms:
                t2 = OP('vector', lambda h, a=sq, o=vr, iw=1.0 / w, e=eps:
                        h.tensor_scalar(o, a, iw, e, ALU.mult, ALU.add), [t1])
            else:
                t0 = OP('vector', lambda h, s=src, o=sm:
                        h.reduce_sum(o, s, axis=AX.X), dd)
                tmu = OP('vector', lambda h, a=sm, o=mu_ap, iw=1.0 / w:
                         h.tensor_scalar(o, a, iw, None, ALU.mult), [t0])
                tm2 = OP('vector', lambda h, a=mu_ap, o=m2:
                         h.tensor_mul(o, a, a), [tmu])
                tm2 = OP('vector', lambda h, o=m2, e=eps:
                         h.tensor_scalar(o, o, e, None, ALU.subtract), [tm2])
                t2 = OP('vector', lambda h, a=sq, b=m2, o=vr, iw=1.0 / w:
                        h.scalar_tensor_tensor(o, a, iw, b, ALU.mult, ALU.subtract),
                        [t1, tm2])
            t3 = OP('scalar', lambda h, a=vr, o=sd:
                    h.activation(o, a, AF.Sqrt), [t2])
            t4 = OP('vector', lambda h, a=sd, o=r_ap: h.reciprocal(o, a), [t3])
            stats_chain[tset] = t4
            return t4

        def quant_act(src, out_bf, inv_ap, w, scrA, scrB, deps, rm_ap=None, tset=0):
            b0 = 186 if tset else 197
            am = STATS[:, b0:b0 + 1]
            cl = STATS[:, b0 + 1:b0 + 2]
            cf = STATS[:, b0 + 2:b0 + 3]
            dd = list(deps) + ([stats_chain[tset]] if stats_chain[tset] else [])
            t0 = OP('vector', lambda h, s=src, o=am:
                    h.reduce_max(o, s, axis=AX.X, apply_absolute_value=True), dd)
            if rm_ap is not None:
                t0 = OP('vector', lambda h, o=am, a=rm_ap: h.tensor_mul(o, o, a), [t0])
            t1 = OP('vector', lambda h, a=am, o=cl:
                    h.tensor_scalar(o, a, 1e-5, None, ALU.max), [t0])
            t2 = OP('vector', lambda h, a=cl, o=inv_ap:
                    h.tensor_scalar(o, a, 1.0 / 127.0, None, ALU.mult), [t1])
            t3 = OP('vector', lambda h, a=cl, o=cf: h.reciprocal(o, a), [t1])
            t4 = OP('vector', lambda h, o=cf: h.tensor_scalar(o, o, 127.0, None, ALU.mult), [t3])
            if rm_ap is not None:
                t4 = OP('vector', lambda h, o=cf, a=rm_ap: h.tensor_mul(o, o, a), [t4])
            stats_chain[tset] = t4
            iw = scrA.bitcast(I32)
            tV = OP('vector', lambda h, s=src, o=scrB, a=cf:
                    h.tensor_scalar(o, s, a, 0.5, ALU.mult, ALU.add), [t4] + list(deps))
            ti = OP('vector', lambda h, s=scrB, o=iw: h.tensor_copy(o, s), [tV])
            tf = OP('vector', lambda h, s=iw, o=scrA: h.tensor_copy(o, s), [ti])
            td = OP('vector', lambda h, a=scrA, b=scrB, o=scrB:
                    h.tensor_tensor(o, a, b, ALU.is_gt), [tf])
            tr = OP('vector', lambda h, a=scrA, b=scrB, o=scrA:
                    h.tensor_sub(o, a, b), [td])
            tq = OP('vector', lambda h, a=scrA, o=out_bf:
                    h.tensor_scalar(o, a, -128.0, 127.0, ALU.max, ALU.min), [tr])
            return tq, ('p:vector', P.cnt['p:vector'])

        def prep_weight(wd, R, C, out_t, scol, deps=(), acc_deps=()):
            acc = STATS[:, scol + 2:scol + 3]
            tacc = None
            for d_ in acc_deps:
                if d_ is not None:
                    tacc = d_
            tacc = tacc or ('p:vector', 0)
            tmm = OP('tensor', lambda h: h.matmul(PS[0:1, 7 * 512:7 * 512 + 1],
                                                  acc, ONE[:, 0:1],
                                                  start=True, stop=True),
                     [tacc] + list(acc_deps[1:]) + bank_dep(7))
            tot = STATS[0:1, scol + 5:scol + 6]
            te = OP('vector', lambda h, o=tot: h.tensor_copy(o, PS[0:1, 7 * 512:7 * 512 + 1]), [tmm])
            set_bank(7, te)
            mcl = STATS[0:1, scol + 6:scol + 7]
            t1 = OP('vector', lambda h, a=tot, o=mcl, iw=1.0 / (R * C):
                    h.tensor_scalar(o, a, iw, 1e-5, ALU.mult, ALU.max), [te])
            sS = STATS[0:1, scol + 7:scol + 8]
            t2 = OP('vector', lambda h, a=mcl, o=sS: h.reciprocal(o, a), [t1])
            tb = OP('tensor', lambda h, a=sS: h.matmul(PS[:, 7 * 512:7 * 512 + 1],
                                                       ONE[0:1, 0:128], a,
                                                       start=True, stop=True),
                    [t2] + bank_dep(7))
            sbc = STATS[:, scol:scol + 1]
            tsb = OP('vector', lambda h, o=sbc: h.tensor_copy(o, PS[:, 7 * 512:7 * 512 + 1]), [tb])
            set_bank(7, tsb)
            tsb = OP('vector', lambda h, o=STATS[:, scol + 1:scol + 2], a=sbc:
                     h.reciprocal(o, a), [tsb])
            tprev = [tsb] + list(deps)
            slot_use2 = {}
            pi_ = 0
            for jg in range(R // 512):
                for cg in range(C // 1024):
                    par2 = pi_ & 1
                    pi_ += 1
                    LB = WS if par2 == 0 else PTS
                    tls = []
                    for jj in range(4):
                        j = jg * 4 + jj
                        tls.append(LD(LB[:, ts(jj, 1024)], wd[ts(j, 128), ts(cg, 1024)],
                                      slot_use2.get(par2, tprev), ch=8 + par2 * 7))
                    tq4 = []
                    for jj in range(4):
                        tm1 = OP('vector', lambda h, s=LB[:, ts(jj, 1024)], o=S1[:, ts(jj, 1024)], a=sbc:
                                 h.tensor_scalar(o, s, a, 0.5, ALU.mult, ALU.is_ge),
                                 [tls[-1]] + tprev[-1:])
                        tm2 = OP('vector', lambda h, s=LB[:, ts(jj, 1024)], o=FM[:, ts(jj, 1024)], a=sbc:
                                 h.tensor_scalar(o, s, a, -0.5, ALU.mult, ALU.is_lt),
                                 [tls[-1]] + tprev[-1:])
                        tq = OP('vector', lambda h, a=S1[:, ts(jj, 1024)], b=FM[:, ts(jj, 1024)],
                                o=QBF[:, ts(jj, 1024)]: h.tensor_sub(o, a, b), [tm1, tm2])
                        tq4.append(tq)
                    slot_use2[par2] = [tq4[-1]]
                    tprev = []
                    for cs in range(8):
                        tev = []
                        for jj in range(4):
                            tev.append(trans128(QBF[:, jj * 1024 + cs * 128: jj * 1024 + cs * 128 + 128],
                                                XQT[:, ts(jj, 128)], 6 + (cs & 1),
                                                deps=[tq4[jj]] + bank_dep(77), bf=True))
                        row = cg * 1024 + cs * 128
                        tst = ST(bass.AP(out_t, row * R + jg * 512, [[R, 128], [1, 512]]),
                                 XQT[:, 0:512], tev)
                        bank_tok[77] = tst
                        tprev.append(tst)
            return tprev

        def load_cols(dram2d, row0, n, col0, scale=None, deps=()):
            lts = [LD(STATS[:, col0 + c:col0 + c + 1], dram2d[row0 + c:row0 + c + 1, :],
                      deps, ch=9) for c in range(n)]
            if scale is None:
                return [lts[-1]]
            toks = []
            for c in range(n):
                toks.append(OP('vector', lambda h, o=STATS[:, col0 + c:col0 + c + 1], s=scale:
                               h.tensor_scalar(o, o, s, None, ALU.mult), [lts[-1]]))
            return [toks[-1]]

        def norm_probe(psum_ap, ncol, scr, deps):
            n = psum_ap.shape[-1]
            t1 = OP('scalar', lambda h, s=psum_ap, o=scr[:, 0:n]:
                    h.activation(o, s, AF.Square), list(deps) + bank_dep(79))
            t2 = OP('tensor', lambda h, s=scr[:, 0:n], nn=n:
                    h.matmul(PS[0:2, 7 * 512:7 * 512 + nn], STATS[:, O2:O2 + 2], s,
                             start=True, stop=True), [t1] + bank_dep(7))
            bank_tok[79] = t2
            tmp = STATS[0:2, BSCR + 9:BSCR + 10]
            t3 = OP('vector', lambda h, o=tmp, nn=n:
                    h.reduce_max(o, PS[0:2, 7 * 512:7 * 512 + nn], axis=AX.X), [t2])
            set_bank(7, t3)
            t4 = OP('vector', lambda h, o=STATS[0:2, ncol:ncol + 1], a=tmp:
                    h.tensor_max(o, o, a), [t3])
            return t4

        def norm_fin(ncol, out11, deps):
            t1 = P.dma('gpsimd', lambda h: h.dma_start(
                out=STATS[0:1, out11 + 1:out11 + 3], in_=STATS[0:2, ncol:ncol + 1]), deps)
            t2 = OP('vector', lambda h: h.reduce_max(
                STATS[0:1, out11:out11 + 1], STATS[0:1, out11 + 1:out11 + 3], axis=AX.X), [t1])
            return t2

        def proj_fm(wdram, wcol0, nhe, fm_cols, writer, scale_col=None,
                    bias_col0=None, ncol=None, nscr=None, deps=(), aux=()):
            tw_prev = list(deps)
            slot_use = {}
            for hc in range(nhe // 128):
                sb_ = 2048 + (hc & 1) * 1024
                chn = 8 + (hc & 1) * 7
                tl = []
                for c in range(NC):
                    tl.append(LD(WS[:, sb_ + c * 128: sb_ + (c + 1) * 128],
                                 wdram[ts(c, 128), wcol0 + hc * 128: wcol0 + (hc + 1) * 128],
                                 slot_use.get(hc & 1, tw_prev), ch=chn))
                tws = [tl[-1]]
                if scale_col is not None:
                    tws = []
                    for c in range(NC):
                        tws.append(OP('vector', lambda h, o=WS[:, sb_ + c * 128:sb_ + (c + 1) * 128],
                                      s=STATS[:, scale_col + c:scale_col + c + 1]:
                                      h.tensor_scalar(o, o, s, None, ALU.mult),
                                      [tl[-1]] + list(aux)))
                ntt = (fm_cols[1] - fm_cols[0]) // (NC * 512)
                tw_prev = []
                for tt_ in range(ntt):
                    b = 4 + (tt_ & 1)
                    pairs = [(WS[:, sb_ + c * 128: sb_ + (c + 1) * 128],
                              FM[:, fm_cols[0] + c * (fm_cols[1] - fm_cols[0]) // NC + tt_ * 512:
                                 fm_cols[0] + c * (fm_cols[1] - fm_cols[0]) // NC + (tt_ + 1) * 512])
                             for c in range(NC)]
                    tm = mm_group(bank(b), pairs, tws + bank_dep(b))
                    pa = bank(b)
                    dd = [tm]
                    if bias_col0 is not None:
                        tb = OP('vector', lambda h, p=pa, bc=STATS[:, bias_col0 + hc:bias_col0 + hc + 1]:
                                h.tensor_scalar(p, p, bc, None, ALU.add), [tm] + list(aux))
                        dd = [tb]
                    if ncol is not None:
                        dd.append(norm_probe(pa, ncol, nscr, dd))
                    tvw = writer(hc, tt_, pa, dd)
                    set_bank(b, tvw)
                    tw_prev.append(tvw)
                slot_use[hc & 1] = list(tw_prev)
            return tw_prev

        def proj_tm(rhs_dram, rcol0, nout, lhsT_fn, nbt_, writer, bias_row=None,
                    scale_col=None, deps=(), aux=()):
            tw_prev = list(deps)
            for nt in range(nout // 512):
                tl = []
                for c in range(NC):
                    tl.append(LD(WS[:, ts(c, 512)],
                                 rhs_dram[ts(c, 128), rcol0 + nt * 512: rcol0 + (nt + 1) * 512],
                                 tw_prev, ch=8))
                tws = [tl[-1]]
                if scale_col is not None:
                    tws = []
                    for c in range(NC):
                        tws.append(OP('vector', lambda h, o=WS[:, ts(c, 512)],
                                      s=STATS[:, scale_col + c:scale_col + c + 1]:
                                      h.tensor_scalar(o, o, s, None, ALU.mult),
                                      [tl[-1]] + list(aux)))
                tbr = []
                bcol = 3072 + (nt & 1) * 512
                if bias_row is not None:
                    tbr = [LD(PTS[0:1, bcol:bcol + 512],
                              bias_row[0:1, nt * 512:(nt + 1) * 512],
                              tw_prev + bank_dep(81 + (nt & 1)), ch=9)]
                tw_prev = []
                for btl in range(nbt_):
                    b = 4 + (btl & 1)
                    pairs = [(lhsT_fn(c, btl), WS[:, ts(c, 512)]) for c in range(NC)]
                    tm = mm_group(bank(b), pairs, tws + bank_dep(b),
                                  stop=(bias_row is None))
                    if bias_row is not None:
                        tm = bias_mm(bank(b), PTS[0:1, bcol:bcol + 512], [tm] + tbr)
                        bank_tok[81 + (nt & 1)] = tm
                    tvw = writer(btl, nt, bank(b), [tm])
                    set_bank(b, tvw)
                    tw_prev.append(tvw)
            return tw_prev

        for bt in range(NBT):
            slot = bt & 1
            xs = X[:, ts(bt, 1024)]
            xn = S1[:, ts(slot, 1024)]
            tst = ln_stats(xs, STATS[:, MU0 + 2 * bt:MU0 + 2 * bt + 1],
                           STATS[:, MU0 + 2 * bt + 1:MU0 + 2 * bt + 2],
                           D, 1e-5, False, S1[:, 2048:3072], [t_x[bt]])
            txn = OP('vector', lambda h, s=xs, o=xn,
                     mu=STATS[:, MU0 + 2 * bt:MU0 + 2 * bt + 1],
                     r=STATS[:, MU0 + 2 * bt + 1:MU0 + 2 * bt + 2]:
                     h.tensor_scalar(o, s, mu, r, ALU.subtract, ALU.mult),
                     [tst] + bank_dep(90 + slot))
            tevs = []
            for c in range(NC):
                tevs.append(trans128(xn[:, ts(c, 128)], FM[:, c * 2048 + bt * 128:c * 2048 + (bt + 1) * 128],
                                     (c & 1), deps=[txn]))
            bank_tok[90 + slot] = tevs[-1]

        P.barrier()
        if DBG is not None and DBG[0] == 'xnfd1':
            ST(bass.AP(xnfd_t, 0, [[16384, 128], [1, 16384]]), FM[:, :])
            P.barrier()

        t_enc = [LD(S1[:, 2048 + i * 1024: 2048 + (i + 1) * 1024], enc_slice[ts(i, 128), :])
                 for i in range(2)]
        tenc_ev = []
        for i in range(2):
            for c in range(NC):
                tenc_ev.append(trans128(S1[:, 2048 + i * 1024 + c * 128: 2048 + i * 1024 + (c + 1) * 128],
                               S1[:, c * 256 + i * 128: c * 256 + (i + 1) * 128],
                               (c & 1), deps=[t_enc[i]]))
        for _c in range(160, 166):
            OP('vector', lambda h, c=_c: h.memset(STATS[0:2, c:c + 1], 0.0))
        tpad = OP('vector', lambda h: h.memset(STATS[0:1, 120:156], 0.0))
        txkb = load_cols(xkb, 0, 8, 256)

        kx_done = []

        def kx_writer(hc, tt_, pa, dd):
            st_ = QH[:, ts(tt_ & 1, 512)][:, 0:256]
            te = COPY(evac_eng(), st_, pa, dd)
            tw = ST(bass.AP(kvx_in_t, hc * 128 * 256, [[256, 128], [1, 256]]), st_, [te])
            kx_done.append(tw)
            return tw

        tw_prev = []
        for hc in range(8):
            tws = []
            for c in range(NC):
                t = LD(WS[:, 2048 + c * 128:2048 + (c + 1) * 128],
                       xkw[ts(c, 128), hc * 128:(hc + 1) * 128], tw_prev, ch=8)
                tws.append(t)
            b = 4 + (hc & 1)
            pairs = [(WS[:, 2048 + c * 128:2048 + (c + 1) * 128], S1[:, ts(c, 256)])
                     for c in range(NC)]
            tm = mm_group(bank(b, 256), pairs, tws + bank_dep(b) + tenc_ev)
            tb = OP('vector', lambda h, p=bank(b, 256), bc=STATS[:, 256 + hc:257 + hc]:
                    h.tensor_scalar(p, p, bc, None, ALU.add), [tm] + txkb)
            tn = norm_probe(bank(b, 256), 160, QH[:, 1024:1536], [tb])
            tw = kx_writer(hc, hc, bank(b, 256), [tb, tn])
            set_bank(b, tw)
            tw_prev = [tw]
        tnf = norm_fin(160, 168, [P.now('vector')])
        tnc = OP('vector', lambda h: h.tensor_copy(STATS[0:1, 120:121],
                                                   STATS[0:1, 168:169]), [tnf, tpad])
        kx_done.append(ST(bass.AP(kvx_in_t, 2 * 256 * D, [[1, 16], [1, 1]]),
                          STATS[0:1, 120:136], [tnc]))
        xkbv = bass.AP(xkb.tensor, 8 * 128, [[1024, 1], [1, 1024]])
        tvb = LD(PTS[0:1, 3072:4096], xkbv, ch=9)

        def vx_writer(btl, nt, pa, dd):
            st_ = QH[:, ts(btl & 1, 512)]
            te = COPY(evac_eng(), st_, pa, dd)
            tw = ST(bass.AP(kvx_in_t, 2 * 256 * D // 2 + btl * 128 * D + nt * 512,
                            [[D, 128], [1, 512]]), st_, [te])
            kx_done.append(tw)
            return tw

        tw_prev = [tw]
        for nt in range(2):
            tws = [LD(WS[:, ts(c, 512)], xkw[ts(c, 128), D + nt * 512:D + (nt + 1) * 512],
                      tw_prev, ch=8) for c in range(NC)]
            tw_prev = []
            for btl in range(2):
                b = 4 + (btl & 1)
                pairs = [(S1[:, c * 256 + btl * 128: c * 256 + (btl + 1) * 128], WS[:, ts(c, 512)])
                         for c in range(NC)]
                tm = mm_group(bank(b), pairs, tws + bank_dep(b) + tenc_ev, stop=False)
                tm = bias_mm(bank(b), PTS[0:1, 3072 + nt * 512:3072 + (nt + 1) * 512], [tm, tvb])
                tw = vx_writer(btl, nt, bank(b), [tm])
                set_bank(b, tw)
                tw_prev.append(tw)
        tvx_done = list(tw_prev)

        t_cc3 = P.cc(lambda h: h.collective_compute(
            "AllGather", ALU.bypass, replica_groups=[list(range(8))],
            ins=[kvx_in_t.ap().opt()], outs=[kvx_out_t.ap().opt()]), kx_done)

        tns = load_cols(ns0, 0, 8, 272)
        for c in range(8):
            OP('vector', lambda h, o=STATS[:, 280 + c:281 + c], s=STATS[:, 272 + c:273 + c]:
               h.tensor_scalar(o, s, 0.125, None, ALU.mult), tns)

        kv0_done = []

        def mk_kv_writer(kv_t, done):
            def wr(hc, tt_, pa, dd):
                st_ = S1[:, 3072 + (tt_ & 1) * 512: 3072 + ((tt_ & 1) + 1) * 512]
                te = COPY(evac_eng(), st_, pa, dd + bank_dep(92 + (tt_ & 1)))
                tw = ST(bass.AP(kv_t, hc * 128 * T + tt_ * 512, [[T, 128], [1, 512]]), st_, [te])
                bank_tok[92 + (tt_ & 1)] = tw
                done.append(tw)
                return tw
            return wr

        def mk_vwriter(kv_t, done):
            def wr(btl, nt, pa, dd):
                st_ = S1[:, 3072 + (btl & 1) * 512: 3072 + ((btl & 1) + 1) * 512]
                te = COPY(evac_eng(), st_, pa, dd + bank_dep(92 + (btl & 1)))
                tw = ST(bass.AP(kv_t, T * D + btl * 128 * D + nt * 512, [[D, 128], [1, 512]]), st_, [te])
                bank_tok[92 + (btl & 1)] = tw
                done.append(tw)
                return tw
            return wr

        tk0 = proj_fm(wk0, 0, D, (0, 16384), mk_kv_writer(kv0_in_t, kv0_done),
                      scale_col=272, ncol=161, nscr=S1[:, 2048:2560], aux=tns,
                      deps=tvx_done)
        tnf = norm_fin(161, 169, [P.now('vector')])
        tnc0 = OP('vector', lambda h: h.tensor_copy(STATS[0:1, 121:122],
                                                    STATS[0:1, 169:170]), [tnf, tpad])
        kv0_done.append(ST(bass.AP(kv0_in_t, 2 * T * D, [[1, 32], [1, 1]]),
                           STATS[0:1, 121:153], [tnc0]))
        tv0 = proj_tm(wv0, 0, D, lambda c, btl: FM[:, c * 2048 + btl * 128: c * 2048 + (btl + 1) * 128],
                      NBT, mk_vwriter(kv0_in_t, kv0_done), scale_col=272, deps=tk0, aux=tns)
        t_cc1 = P.cc(lambda h: h.collective_compute(
            "AllGather", ALU.bypass, replica_groups=[[0, 1], [2, 3], [4, 5], [6, 7]],
            ins=[kv0_in_t.ap().opt()], outs=[kv0_out_t.ap().opt()]), kv0_done)

        def mk_qt_writer(qt_t):
            def wr(hc, tt_, pa, dd):
                st_ = S1[:, 3072 + (tt_ & 1) * 512: 3072 + ((tt_ & 1) + 1) * 512]
                te = COPY(evac_eng(), st_, pa, dd + bank_dep(92 + (tt_ & 1)))
                tw = ST(bass.AP(qt_t, hc * 128 * T + tt_ * 512, [[T, 128], [1, 512]]), st_, [te])
                bank_tok[92 + (tt_ & 1)] = tw
                return tw
            return wr

        proj_fm(wq0, 0, D, (0, 16384), mk_qt_writer(qt0_t),
                scale_col=280, ncol=162, nscr=S1[:, 2048:2560], deps=tv0,
                aux=[P.now('vector')])
        norm_fin(162, 170, [P.now('vector')])

        P.barrier()

        if DBG is not None and DBG[0] == 'xnfd':
            ST(bass.AP(xnfd_t, 0, [[16384, 128], [1, 16384]]), FM[:, :])
            P.barrier()
        if DBG is not None and DBG[0] == 'qt0e':
            ST(bass.AP(xnfd_t, 0, [[2048, 1024], [1, 2048]]),
               bass.AP(qt0_t, 0, [[2048, 1024], [1, 2048]]))
            P.barrier()
        tp = prep_weight(w1, INNER, D, w1q_t, 288, acc_deps=p1_toks[288])
        tp = prep_weight(w2, D, INNER, w2q_t, 296, deps=tp, acc_deps=p1_toks[296])
        tp = prep_weight(wg, INNER, D, wgq_t, 304, deps=tp, acc_deps=p1_toks[304])
        tp = prep_weight(wu, INNER, D, wuq_t, 312, deps=tp, acc_deps=p1_toks[312])
        prep_weight(wd, D, INNER, wdq_t, 320, deps=tp, acc_deps=p1_toks[320])
        P.barrier()

        def mk_bound(mq_col, mk_col, out_bv_col, extra_col=None):
            a = STATS[0:1, BSCR:BSCR + 1]
            b_ = STATS[0:1, BSCR + 1:BSCR + 2]
            t1 = OP('scalar', lambda h, s=STATS[0:1, mq_col:mq_col + 1], o=a:
                    h.activation(o, s, AF.Sqrt), [P.now('vector'), P.now('gpsimd')])
            t2 = OP('scalar', lambda h, s=STATS[0:1, mk_col:mk_col + 1], o=b_:
                    h.activation(o, s, AF.Sqrt), [P.now('vector'), P.now('gpsimd')])
            t3 = OP('vector', lambda h, o=a: h.tensor_mul(o, a, b_), [t1, t2])
            if extra_col is not None:
                t3 = OP('vector', lambda h, o=a, e=STATS[0:1, extra_col:extra_col + 1]:
                        h.tensor_add(o, o, e), [t3])
            t4 = OP('vector', lambda h, o=a: h.tensor_scalar(o, a, -1.0, None, ALU.mult), [t3])
            t5 = OP('tensor', lambda h, s=a: h.matmul(PS[:, 7 * 512:7 * 512 + 1],
                                                      ONE[0:1, 0:128], s,
                                                      start=True, stop=True),
                    [t4] + bank_dep(7))
            t6 = OP('vector', lambda h, o=STATS[:, out_bv_col:out_bv_col + 1]:
                    h.tensor_copy(o, PS[:, 7 * 512:7 * 512 + 1]), [t5])
            set_bank(7, t6)
            return t6

        def attention(kind, kv_t, shard, qt_t, wo_d, bias_row_d, bvec_fn, epi, dep0,
                      bdeps=()):
            nkb = 32 if kind in ('l0', 'self') else 16
            KVB_t = KVB[:, :].tensor
            NSL = 3

            def k_ap(h_, kb):
                if kind == 'cross':
                    off = (kb >> 1) * KVX_SHARD + h_ * 64 * 256 + (kb & 1) * 128
                    return bass.AP(kv_t, off, [[256, 64], [1, 128]])
                off = (kb & 1) * shard + h_ * 64 * T + (kb >> 1) * 128
                return bass.AP(kv_t, off, [[T, 64], [1, 128]])

            def v_ap(h_, kb):
                if kind == 'cross':
                    off = (kb >> 1) * KVX_SHARD + 256 * D + (kb & 1) * 128 * D + h_ * 64
                    return bass.AP(kv_t, off, [[D, 128], [1, 64]])
                off = (kb & 1) * shard + T * D + (kb >> 1) * 128 * D + h_ * 64
                return bass.AP(kv_t, off, [[D, 128], [1, 64]])

            def kb_spec(qg):
                out = []
                for kb in range(nkb):
                    masks = []
                    if kind == 'l0':
                        out.append((kb, 0, masks))
                        continue
                    if kind == 'self':
                        if qg == 0 and kb >= 16:
                            continue
                        t0l = kb // 2 - 8 * qg
                        c0 = max(0, t0l) * 128
                        if 0 <= t0l <= 7:
                            masks.append((bass.AP(mask_self_d.tensor, kb * 16384, [[128, 128], [1, 128]]), t0l * 128))
                        out.append((kb, c0, masks))
                    else:
                        t0 = max(0, (kb - 1) // 2)
                        c0 = max(0, t0 - 8 * qg) * 128
                        for j in range(2):
                            tl = t0 + j - 8 * qg
                            if 0 <= tl <= 7:
                                masks.append((bass.AP(mask_cross_d.tensor, (kb * 2 + j) * 16384, [[128, 128], [1, 128]]), tl * 128))
                        out.append((kb, c0, masks))
                return out

            tv_ones = []
            for s_ in range(2):
                for l_ in range(2):
                    c = 1024 + s_ * 130 + l_ * 65 + 64
                    tv_ones.append(OP('vector', lambda h, c=c: h.memset(KVB[:, c:c + 1], 1.0)))

            def k_gap(h_, kb0):
                if kind == 'cross':
                    off = (kb0 >> 1) * KVX_SHARD + h_ * 64 * 256
                    return bass.AP(kv_t, off, [[256, 64], [128, 2], [1, 128]])
                off = h_ * 64 * T + (kb0 >> 1) * 128
                return bass.AP(kv_t, off, [[T, 64], [shard, 2], [1, 128]])

            def v_gap(h_, kb0):
                if kind == 'cross':
                    off = (kb0 >> 1) * KVX_SHARD + 256 * D + (kb0 & 1) * 128 * D + h_ * 64
                    return bass.AP(kv_t, off, [[D, 128], [128 * D, 2], [1, 64]])
                off = T * D + (kb0 >> 1) * 128 * D + h_ * 64
                return bass.AP(kv_t, off, [[D, 128], [shard, 2], [1, 64]])
            ep_toks = []
            for qg in range(2):
                t_q = [None]
                if ep_toks:
                    slot_S = dict.fromkeys(range(NSL), ep_toks[-1])
                else:
                    slot_S = {}
                slot_PT = dict.fromkeys(range(NSL), dep0[0]) if dep0 else {}
                slot_K = dict.fromkeys(range(NSL), dep0[0]) if dep0 else {}
                mask_tok = {}
                a_dep = bank_dep(0) + bank_dep(1) + tv_ones
                for h_ in range(NH):
                    tq = P.dma('sync', lambda h, d=QH[0:64, 0:1024],
                               sr=bass.AP(qt_t, h_ * 64 * T + qg * 1024, [[T, 64], [1, 1024]]):
                               h.dma_start(out=d, in_=sr),
                               ([t_q[0]] if t_q[0] else []) + list(dep0), ch=7)
                    spec = kb_spec(qg)
                    last_b0 = max((i for i, (kb, c0, m) in enumerate(spec) if c0 < 512),
                                  default=None)
                    pend = []
                    last_S = [None]

                    def flush_av():
                        if pend:
                            f, dd, sl = pend.pop(0)
                            tok = P.do('tensor', f, dd)
                            slot_PT[sl] = tok
                            for gs_ in list(gV):
                                gV[gs_] = (gV[gs_][0], tok)
                            return tok
                        return None

                    gK = {}
                    gV = {}
                    for idx, (kb, c0, masks) in enumerate(spec):
                        slot = idx % NSL
                        mslot = idx & 1
                        if idx % 2 == 0:
                            gslot = (idx // 4) & 1
                            half = (idx // 2) & 1
                            tkg = P.dma('sync', lambda h,
                                        d=bass.AP(KVB_t, gslot * 512 + half * 256,
                                                  [[1288, 64], [128, 2], [1, 128]]),
                                        sr=k_gap(h_, kb):
                                        h.dma_start(out=d, in_=sr),
                                        ([gK[gslot][1]] if gslot in gK and gK[gslot][1]
                                         else list(dep0)), ch=1 + gslot)
                            gK[gslot] = (tkg, gK[gslot][1] if gslot in gK else None)
                        if idx % 2 == 0:
                            vgs = (idx // 2) & 1
                            tvg = P.dma('sync', lambda h,
                                        d=bass.AP(KVB_t, 1024 + vgs * 130, [[1288, 128], [65, 2], [1, 64]]),
                                        sr=v_gap(h_, kb):
                                        h.dma_start(out=d, in_=sr),
                                        ([gV[vgs][1]] if vgs in gV and gV[vgs][1]
                                         else list(dep0)), ch=4 + vgs)
                            gV[vgs] = (tvg, None)
                        gslot = (idx // 4) & 1
                        vgs = (idx // 2) & 1
                        lane = idx % 4
                        kap = KVB[0:64, gslot * 512 + lane * 128: gslot * 512 + (lane + 1) * 128]
                        vap = KVB[:, 1024 + vgs * 130 + (idx % 2) * 65: 1024 + vgs * 130 + (idx % 2 + 1) * 65]
                        tk = gK[gslot][0]
                        tv = gV[vgs][0]
                        tms = []
                        for mi, (mad, mcol) in enumerate(masks):
                            tms.append(P.dma('sync', lambda h,
                                             d=S1[:, (mslot * 2 + mi) * 128:
                                                   (mslot * 2 + mi + 1) * 128],
                                             sr=mad: h.dma_start(out=d, in_=sr),
                                             ([mask_tok[mslot]] if mslot in mask_tok else []),
                                             ch=12 + mslot))
                        if len(pend) >= 2:
                            flush_av()
                        sb = 2 + 2 * slot
                        segs = ([(sb, c0, 512)] if c0 < 512 else []) + \
                               [(sb + 1, max(c0, 512), 1024)]
                        sdeps = [tk, tq] + ([slot_S[slot]] if slot in slot_S else [])
                        def s_fn(h, segs=segs, kap=kap):
                            last = None
                            for (b_, a0, a1) in segs:
                                last = h.matmul(PS[:, b_ * 512 + (a0 % 512):b_ * 512 + (a0 % 512) + (a1 - a0)],
                                                kap, QH[0:64, a0:a1], start=True, stop=True)
                            return last
                        tS = P.do('tensor', s_fn, sdeps)
                        last_S[0] = tS
                        slot_K[slot] = tS
                        gK[gslot] = (gK[gslot][0], tS)
                        tlast = tS
                        for mi, (mad, mcol) in enumerate(masks):
                            msl = S1[:, (mslot * 2 + mi) * 128: (mslot * 2 + mi + 1) * 128]
                            b_ = sb if mcol < 512 else sb + 1
                            pa = PS[:, b_ * 512 + (mcol % 512): b_ * 512 + (mcol % 512) + 128]
                            tlast = OP('vector', lambda h, p=pa, m=msl:
                                       h.tensor_add(p, p, m), [tlast, tms[mi]])
                            mask_tok[mslot] = tlast
                        bv = bvec_fn(kb)
                        def e_fn(h, slot=slot, c0=c0, bv=bv, sb=sb):
                            return h.activation(PTS[:, slot * 1024 + c0:(slot + 1) * 1024],
                                                PS[:, sb * 512 + c0: sb * 512 + 1024],
                                                AF.Exp, bias=bv)
                        tE = P.do('scalar', e_fn,
                                  [tlast] + list(bdeps) +
                                  ([slot_PT[slot]] if slot in slot_PT else []))
                        slot_S[slot] = tE
                        first = (idx == 0)
                        lastk = (idx == len(spec) - 1)
                        stop0 = (idx == last_b0)
                        def av_fn(h, slot=slot, c0=c0, first=first, lastk=lastk,
                                  stop0=stop0, vap=vap):
                            last = None
                            for (b_, a0, a1) in ([(0, c0, 512)] if c0 < 512 else []) + [(1, max(c0, 512), 1024)]:
                                last = h.matmul(PS[0:65, b_ * 512 + (a0 % 512):b_ * 512 + (a0 % 512) + (a1 - a0)],
                                                vap,
                                                PTS[:, slot * 1024 + a0:slot * 1024 + a1],
                                                start=first,
                                                stop=(stop0 if b_ == 0 else lastk))
                            return last
                        pend.append((av_fn, [tE, tv] + (a_dep if first else []), slot))
                    flush_av()
                    tAV = flush_av()
                    trc = OP('vector', lambda h: h.reciprocal(QH[64:65, 1024:2048],
                                                              PS[64:65, 0:1024]), [tAV])
                    def bc_fn(h):
                        h.matmul(bank(6), ONE[64:65, 0:128], QH[64:65, 1024:1536],
                                 start=True, stop=True)
                        return h.matmul(bank(7), ONE[64:65, 0:128], QH[64:65, 1536:2048],
                                        start=True, stop=True)
                    tbc = P.do('tensor', bc_fn, [trc] + bank_dep(6) + bank_dep(7) +
                               [t for t in slot_S.values()])
                    tbe = OP('vector', lambda h: h.tensor_copy(QH[:, 1024:2048],
                                                               PS[:, 6 * 512:8 * 512]), [tbc])
                    set_bank(6, tbe)
                    set_bank(7, tbe)
                    dst = FM[(h_ % 2) * 64:(h_ % 2) * 64 + 64, (h_ // 2) * 1024:(h_ // 2 + 1) * 1024]
                    tnm = OP('vector', lambda h, d=dst: h.tensor_tensor(
                        d, PS[0:64, 0:1024], QH[0:64, 1024:2048], ALU.mult), [tbe])
                    a_dep = [tnm]
                    bank_tok[0] = tnm
                    bank_tok[1] = tnm
                    t_q[0] = last_S[0]
                for Dt in range(2):
                    tws = [LD(WS[:, ts(c, 512)], wo_d[ts(c, 128), Dt * 512:(Dt + 1) * 512],
                              bank_dep(95), ch=8)
                           for c in range(NC)]
                    tbr = LD(PTS[0:1, 3072 + Dt * 512:3072 + (Dt + 1) * 512],
                             bias_row_d[0:1, Dt * 512:(Dt + 1) * 512], bank_dep(95), ch=9)
                    for btl in range(8):
                        bt = qg * 8 + btl
                        b_ = 6 + (btl & 1)
                        pairs = [(FM[:, c * 1024 + btl * 128: c * 1024 + (btl + 1) * 128],
                                  WS[:, ts(c, 512)]) for c in range(NC)]
                        tm = mm_group(bank(b_), pairs,
                                      tws + [tbr] + bank_dep(b_) + [tnm], stop=False)
                        tm = bias_mm(bank(b_), PTS[0:1, 3072 + Dt * 512:3072 + (Dt + 1) * 512], [tm])
                        tE2 = epi(bt, Dt, bank(b_), [tm])
                        set_bank(b_, tE2)
                        ep_toks.append(tE2)
                    bank_tok[95] = ep_toks[-1]
            return ep_toks

        WSB = WS[:, :].bitcast(BF16)
        R2C, INV1, SC1, INV2, SC2 = 243, 246, 250, 232, 236
        def bcast_row(dst, row_d, n, deps=()):
            toks = []
            for i in range(n // 512):
                tb = LD(QH[0:1, 0:512], row_d[0:1, ts(i, 512)], list(deps) + bank_dep(97), ch=9)
                tm = OP('tensor', lambda h, s=QH[0:1, 0:512]:
                        h.matmul(bank(7), ONE[0:1, 0:128], s, start=True, stop=True),
                        [tb] + bank_dep(7))
                te = COPY(evac_eng(), dst[:, ts(i, 512)], bank(7), [tm])
                set_bank(7, te)
                bank_tok[97] = tm
                toks.append(te)
            return toks

        def l0ffn():
            tb2 = bcast_row(QH[:, 1024:2048], b2, D)
            st_g0 = []
            for g in range(4):
                txq = []
                for bt_l in range(4):
                    bt = g * 4 + bt_l
                    par = bt_l & 1
                    xs = X[:, ts(bt, 1024)]
                    xn = S1[:, ts(par, 1024)]
                    dd = bank_dep(68 + par)
                    txn = OP('vector', lambda h, s=xs, o=xn,
                             mu=STATS[:, MU0 + 2 * bt:MU0 + 2 * bt + 1],
                             r=STATS[:, MU0 + 2 * bt + 1:MU0 + 2 * bt + 2]:
                             h.tensor_scalar(o, s, mu, r, ALU.subtract, ALU.mult), dd)
                    trs = ln_stats(xn, STATS[:, BSCR:BSCR + 1],
                                   STATS[:, R2C + 2 * par:R2C + 2 * par + 1],
                                   D, 1e-6, True, S1[:, 2048 + 1024 * par:3072 + 1024 * par],
                                   [txn], tset=par)
                    tq, _ = quant_act(xn, QBF[:, ts(bt_l, 1024)],
                                      STATS[:, INV1 + bt_l:INV1 + bt_l + 1], D,
                                      PTS[:, ts(par, 1024)],
                                      PTS[:, 2048 + par * 1024:3072 + par * 1024],
                                      [txn, trs], rm_ap=STATS[:, R2C + 2 * par:R2C + 2 * par + 1],
                                      tset=par)
                    bank_tok[68 + par] = tq
                    for c in range(NC):
                        t = trans128(QBF[:, bt_l * 1024 + c * 128: bt_l * 1024 + (c + 1) * 128],
                                     XQT[:, bt_l * 1024 + c * 128: bt_l * 1024 + (c + 1) * 128],
                                     (c & 1), deps=[tq], bf=True)
                    txq.append(t)
                    OP('vector', lambda h, o=STATS[:, SC1 + bt_l:SC1 + bt_l + 1],
                       a=STATS[:, INV1 + bt_l:INV1 + bt_l + 1], b_=STATS[:, 289:290]:
                       h.tensor_mul(o, a, b_), [tq])
                tb1 = bcast_row(PTS[:, 0:4096], b1, INNER, txq)
                tprev = []
                for jt in range(8):
                    tws = [LD(WSB[:, ts(c, 512)],
                              bass.AP(w1q_t, c * 128 * INNER + jt * 512, [[INNER, 128], [1, 512]]),
                              bank_dep(94), ch=8) for c in range(NC)]
                    tprev = []
                    for bt_l in range(4):
                        b_ = 4 + (bt_l & 1)
                        pairs = [(XQT[:, bt_l * 1024 + c * 128: bt_l * 1024 + (c + 1) * 128],
                                  WSB[:, ts(c, 512)]) for c in range(NC)]
                        tm = mm_group(bank(b_), pairs, tws + bank_dep(b_) + txq + tb1)
                        bank_tok[94] = tm
                        th = OP('vector', lambda h, p=bank(b_),
                                o=FM[:, bt_l * 4096 + jt * 512: bt_l * 4096 + (jt + 1) * 512],
                                sc=STATS[:, SC1 + bt_l:SC1 + bt_l + 1],
                                bb=PTS[:, ts(jt, 512)]:
                                h.scalar_tensor_tensor(o, p, sc, bb, ALU.mult, ALU.add), [tm])
                        set_bank(b_, th)
                        tprev.append(th)
                PTSB = PTS[:, :].bitcast(BF16)
                hq = [XQT[:, 0:4096], PTSB[:, 0:4096], PTSB[:, 4096:8192], QBF[:, 0:4096]]
                ttr_all = []
                for bt_l in range(4):
                    hsl = FM[:, ts(bt_l, 4096)]
                    tg = OP('scalar', lambda h, a=hsl: h.activation(a, a, AF.Gelu),
                            tprev + bank_dep(89))
                    trs = ln_stats(hsl, STATS[:, BSCR:BSCR + 1], STATS[:, 228:229],
                                   INNER, 1e-6, True, S1[:, 0:4096], [tg])
                    tq, _ = quant_act(hsl, QBF[:, 0:4096], STATS[:, INV2 + bt_l:INV2 + bt_l + 1],
                                      INNER, hsl, S1[:, 0:4096],
                                      [tg, trs], rm_ap=STATS[:, 228:229])
                    OP('vector', lambda h, o=STATS[:, SC2 + bt_l:SC2 + bt_l + 1],
                       a=STATS[:, INV2 + bt_l:INV2 + bt_l + 1], b_=STATS[:, 297:298]:
                       h.tensor_mul(o, a, b_), [tq])
                    dst = hq[bt_l]
                    ttr = [trans128(QBF[:, ts(c32, 128)], dst[:, ts(c32, 128)],
                                    (c32 & 1), deps=[tq] + bank_dep(93), bf=True)
                           for c32 in range(32)]
                    bank_tok[89] = ttr[-1]
                    bank_tok[93] = ttr[-1]
                    bank_tok[68] = tq
                    bank_tok[69] = tq
                    ttr_all.append(ttr[-1])
                tm_by_bank = {}
                for rnd in range(4):
                    tws = [LD(WSB[:, ts(i, 512)],
                              bass.AP(w2q_t, (rnd * 8 + (i % 8)) * 128 * D + (i // 8) * 512,
                                      [[D, 128], [1, 512]]),
                              bank_dep(94), ch=8) for i in range(16)]
                    for bt_l in range(4):
                        for Dt in range(2):
                            b_ = bt_l * 2 + Dt
                            pairs = [(hq[bt_l][:, (rnd * 8 + c) * 128:(rnd * 8 + c + 1) * 128],
                                      WSB[:, ts(Dt * 8 + c, 512)]) for c in range(NC)]
                            tm = mm_group(bank(b_), pairs,
                                          tws + ttr_all + (bank_dep(b_) if rnd == 0 else []),
                                          start=(rnd == 0), stop=(rnd == 3))
                            tm_by_bank[b_] = tm
                            bank_tok[94] = tm
                            bank_tok[89] = tm
                            bank_tok[97] = tm
                            bank_tok[93] = tm
                            bank_tok[68] = tm
                            bank_tok[69] = tm
                for bt_l in range(4):
                    bt = g * 4 + bt_l
                    for Dt in range(2):
                        b_ = bt_l * 2 + Dt
                        tep = OP('vector', lambda h, p=bank(b_), o=QH[:, 0:512],
                                 sc=STATS[:, SC2 + bt_l:SC2 + bt_l + 1],
                                 bb=QH[:, 1024 + Dt * 512:1024 + (Dt + 1) * 512]:
                                 h.scalar_tensor_tensor(o, p, sc, bb, ALU.mult, ALU.add),
                                 [tm_by_bank[b_]] + tb2 + bank_dep(96))
                        tsl = OP('scalar', lambda h, i_=QH[:, 0:512], o=QH[:, 512:1024]:
                                 h.activation(o, i_, AF.Silu), [tep])
                        set_bank(b_, tsl)
                        tst = ST(bass.AP(g0_t, bt * 128 * D + Dt * 512, [[D, 128], [1, 512]]),
                                 QH[:, 512:1024], [tsl])
                        bank_tok[96] = tst
                        st_g0.append(tst)
            return st_g0

        l0ffn()
        P.barrier()

        tt1 = LD(STATS[0:1, 179:180], bass.AP(kv0_out_t, 2 * T * D, [[1, 1], [1, 1]]), [t_cc1])
        tt2 = LD(STATS[0:1, 180:181], bass.AP(kv0_out_t, KV_SHARD + 2 * T * D, [[1, 1], [1, 1]]), [t_cc1])
        tmx = OP('vector', lambda h: h.reduce_max(STATS[0:1, 181:182], STATS[0:1, 179:181],
                                                  axis=AX.X), [tt1, tt2])
        tB0 = mk_bound(170, 181, BV_L0)

        st_x1 = {}

        def epi_l0(bt, Dt, pa, dd):
            sl = 2048 + (Dt & 1) * 512
            tg = LD(S1[:, sl:sl + 512], bass.AP(g0_t, bt * 128 * D + Dt * 512, [[D, 128], [1, 512]]),
                    bank_dep(88 + Dt), ch=10 + Dt)
            tm1 = OP('vector', lambda h, p=pa, g_=S1[:, sl:sl + 512], o=S1[:, 3072 + Dt * 512:3072 + (Dt + 1) * 512]:
                     h.scalar_tensor_tensor(o, p, 1.0, g_, ALU.bypass, ALU.mult),
                     dd + [tg] + bank_dep(78 + Dt))
            bank_tok[88 + Dt] = tm1
            xs = X[:, bt * 1024 + Dt * 512: bt * 1024 + (Dt + 1) * 512]
            tm2 = OP('vector', lambda h, o=xs, a=S1[:, 3072 + Dt * 512:3072 + (Dt + 1) * 512]:
                     h.tensor_add(o, o, a), [tm1])
            bank_tok[78 + Dt] = tm2
            if Dt == 1:
                st_x1[bt] = ST(x1_t[ts(bt, 128), :], X[:, ts(bt, 1024)], [tm2])
            return tm2

        attention('l0', kv0_out_t, KV_SHARD, qt0_t, wo0, bo0,
                  lambda kb: STATS[:, BV_L0:BV_L0 + 1], epi_l0, [t_cc1], bdeps=[tB0])
        P.barrier()

        for bt in range(NBT):
            xs = X[:, ts(bt, 1024)]
            tst = ln_stats(xs, STATS[:, MU1 + 2 * bt:MU1 + 2 * bt + 1],
                           STATS[:, MU1 + 2 * bt + 1:MU1 + 2 * bt + 2],
                           D, 1e-5, False, S1[:, 2048:3072], [st_x1[bt]])
            OP('vector', lambda h, s=xs,
               mu=STATS[:, MU1 + 2 * bt:MU1 + 2 * bt + 1],
               r=STATS[:, MU1 + 2 * bt + 1:MU1 + 2 * bt + 2]:
               h.tensor_scalar(s, s, mu, r, ALU.subtract, ALU.mult), [tst])
        P.barrier()

        LN1G, LN1B, LNXG, LNXB, LN2G, LN2B = 328, 336, 352, 360, 368, 376
        tln1 = load_cols(ln1g, 0, 8, LN1G)
        tln1 += load_cols(ln1b, 0, 8, LN1B)
        EIGHTH = 78
        OP('vector', lambda h: h.memset(STATS[:, EIGHTH:EIGHTH + 8], 0.125))

        def ln_apply_to_fm(gcol, bcol, gb_deps):
            for bt in range(NBT):
                slot = bt & 1
                xs = X[:, ts(bt, 1024)]
                xn = S1[:, ts(slot, 1024)]
                tst = ln_stats(xs, STATS[:, BSCR:BSCR + 1],
                               STATS[:, BSCR + 1:BSCR + 2],
                               D, 1e-5, False, S1[:, 2048:3072], bank_dep(86 + slot))
                txn = OP('vector', lambda h, s=xs, o=xn,
                         mu=STATS[:, BSCR:BSCR + 1], r=STATS[:, BSCR + 1:BSCR + 2]:
                         h.tensor_scalar(o, s, mu, r, ALU.subtract, ALU.mult), [tst])
                tevs = []
                for c in range(NC):
                    tevs.append(trans128(xn[:, ts(c, 128)],
                                         FM[:, c * 2048 + bt * 128:c * 2048 + (bt + 1) * 128],
                                         (c & 1), deps=[txn] + list(gb_deps),
                                         scale=STATS[:, gcol + c:gcol + c + 1],
                                         bias_ap=STATS[:, bcol + c:bcol + c + 1]))
                bank_tok[86 + slot] = tevs[-1]

        ln_apply_to_fm(LN1G, LN1B, tln1)
        P.barrier()

        kv1_done = []
        tcb1 = load_cols(cab, 0, 8, 384, scale=0.125)
        tq1 = proj_fm(caw, 0, D, (0, 16384), mk_qt_writer(qt1_t),
                      scale_col=EIGHTH, bias_col0=384, ncol=164, nscr=S1[:, 2048:2560],
                      aux=tcb1 + [P.now('vector')])
        norm_fin(164, 172, [P.now('vector')])
        tcb2 = load_cols(cab, 8, 8, 392)
        tk1 = proj_fm(caw, D, D, (0, 16384), mk_kv_writer(kv1_in_t, kv1_done),
                      bias_col0=392, ncol=163, nscr=S1[:, 2048:2560], deps=tq1, aux=tcb2)
        tnf1 = norm_fin(163, 171, [P.now('vector')])
        tnc1 = OP('vector', lambda h: h.tensor_copy(STATS[0:1, 122:123],
                                                    STATS[0:1, 171:172]), [tnf1, tpad])
        kv1_done.append(ST(bass.AP(kv1_in_t, 2 * T * D, [[1, 32], [1, 1]]),
                           STATS[0:1, 122:154], [tnc1]))
        cabv = bass.AP(cab.tensor, 16 * 128, [[1024, 1], [1, 1024]])
        proj_tm(caw, 2 * D, D, lambda c, btl: FM[:, c * 2048 + btl * 128: c * 2048 + (btl + 1) * 128],
                NBT, mk_vwriter(kv1_in_t, kv1_done), bias_row=cabv, deps=tk1)
        t_cc2 = P.cc(lambda h: h.collective_compute(
            "AllGather", ALU.bypass, replica_groups=[[0, 1], [2, 3], [4, 5], [6, 7]],
            ins=[kv1_in_t.ap().opt()], outs=[kv1_out_t.ap().opt()]), kv1_done)
        P.barrier()

        SCU = 260

        def l1ffn():
            st_g1 = []
            for g in range(4):
                txq = []
                for bt_l in range(4):
                    bt = g * 4 + bt_l
                    par = bt_l & 1
                    tq, _ = quant_act(X[:, ts(bt, 1024)], QBF[:, ts(bt_l, 1024)],
                                      STATS[:, INV1 + bt_l:INV1 + bt_l + 1], D,
                                      PTS[:, ts(par, 1024)],
                                      PTS[:, 2048 + par * 1024:3072 + par * 1024],
                                      bank_dep(68 + par), tset=par)
                    bank_tok[68 + par] = tq
                    for c in range(NC):
                        t = trans128(QBF[:, bt_l * 1024 + c * 128: bt_l * 1024 + (c + 1) * 128],
                                     XQT[:, bt_l * 1024 + c * 128: bt_l * 1024 + (c + 1) * 128],
                                     (c & 1), deps=[tq] + bank_dep(93), bf=True)
                    txq.append(t)
                    OP('vector', lambda h, o=STATS[:, SC1 + bt_l:SC1 + bt_l + 1],
                       a=STATS[:, INV1 + bt_l:INV1 + bt_l + 1], b_=STATS[:, 305:306]:
                       h.tensor_mul(o, a, b_), [tq])
                    OP('vector', lambda h, o=STATS[:, SCU + bt_l:SCU + bt_l + 1],
                       a=STATS[:, INV1 + bt_l:INV1 + bt_l + 1], b_=STATS[:, 313:314]:
                       h.tensor_mul(o, a, b_), [tq])
                for jt in range(8):
                    twg = [LD(WSB[:, ts(c, 512)],
                              bass.AP(wgq_t, c * 128 * INNER + jt * 512, [[INNER, 128], [1, 512]]),
                              bank_dep(94), ch=8) for c in range(NC)]
                    twu = [LD(WSB[:, 4096 + c * 512: 4096 + (c + 1) * 512],
                              bass.AP(wuq_t, c * 128 * INNER + jt * 512, [[INNER, 128], [1, 512]]),
                              bank_dep(94), ch=8) for c in range(NC)]
                    for bt_l in range(4):
                        pg = [(XQT[:, bt_l * 1024 + c * 128: bt_l * 1024 + (c + 1) * 128],
                               WSB[:, ts(c, 512)]) for c in range(NC)]
                        pu = [(XQT[:, bt_l * 1024 + c * 128: bt_l * 1024 + (c + 1) * 128],
                               WSB[:, 4096 + c * 512: 4096 + (c + 1) * 512]) for c in range(NC)]
                        tmg = mm_group(bank(4), pg, twg + bank_dep(4) + txq)
                        bank_tok[94] = tmg
                        tmu = mm_group(bank(5), pu, twu + bank_dep(5) + txq)
                        bank_tok[94] = tmu
                        tsl = OP('scalar', lambda h, p=bank(4), o=QH[:, 0:512],
                                 sc=STATS[:, SC1 + bt_l:SC1 + bt_l + 1]:
                                 h.activation(o, p, AF.Silu, scale=sc),
                                 [tmg] + bank_dep(96))
                        set_bank(4, tsl)
                        tup = OP('vector', lambda h, p=bank(5), o=QH[:, 512:1024],
                                 sc=STATS[:, SCU + bt_l:SCU + bt_l + 1]:
                                 h.tensor_scalar(o, p, sc, None, ALU.mult),
                                 [tmu] + bank_dep(96))
                        set_bank(5, tup)
                        tgu = OP('vector', lambda h,
                                 o=FM[:, bt_l * 4096 + jt * 512: bt_l * 4096 + (jt + 1) * 512]:
                                 h.tensor_mul(o, QH[:, 0:512], QH[:, 512:1024]), [tsl, tup])
                        bank_tok[96] = tgu
                PTSB = PTS[:, :].bitcast(BF16)
                hq = [XQT[:, 0:4096], PTSB[:, 0:4096], PTSB[:, 4096:8192], QBF[:, 0:4096]]
                ttr_all = []
                for bt_l in range(4):
                    gsl = FM[:, ts(bt_l, 4096)]
                    tq, _ = quant_act(gsl, QBF[:, 0:4096], STATS[:, INV2 + bt_l:INV2 + bt_l + 1],
                                      INNER, gsl, S1[:, 0:4096],
                                      bank_dep(96) + bank_dep(89))
                    OP('vector', lambda h, o=STATS[:, SC2 + bt_l:SC2 + bt_l + 1],
                       a=STATS[:, INV2 + bt_l:INV2 + bt_l + 1], b_=STATS[:, 321:322]:
                       h.tensor_mul(o, a, b_), [tq])
                    dst = hq[bt_l]
                    ttr = [trans128(QBF[:, ts(c32, 128)], dst[:, ts(c32, 128)],
                                    (c32 & 1), deps=[tq] + bank_dep(93), bf=True)
                           for c32 in range(32)]
                    bank_tok[89] = ttr[-1]
                    bank_tok[93] = ttr[-1]
                    bank_tok[68] = tq
                    bank_tok[69] = tq
                    ttr_all.append(ttr[-1])
                tm_by_bank = {}
                for rnd in range(4):
                    tws = [LD(WSB[:, ts(i, 512)],
                              bass.AP(wdq_t, (rnd * 8 + (i % 8)) * 128 * D + (i // 8) * 512,
                                      [[D, 128], [1, 512]]),
                              bank_dep(94), ch=8) for i in range(16)]
                    for bt_l in range(4):
                        for Dt in range(2):
                            b_ = bt_l * 2 + Dt
                            pairs = [(hq[bt_l][:, (rnd * 8 + c) * 128:(rnd * 8 + c + 1) * 128],
                                      WSB[:, ts(Dt * 8 + c, 512)]) for c in range(NC)]
                            tm = mm_group(bank(b_), pairs,
                                          tws + ttr_all + (bank_dep(b_) if rnd == 0 else []),
                                          start=(rnd == 0), stop=(rnd == 3))
                            tm_by_bank[b_] = tm
                            bank_tok[94] = tm
                            bank_tok[89] = tm
                            bank_tok[93] = tm
                            bank_tok[96] = tm
                            bank_tok[68] = tm
                            bank_tok[69] = tm
                for bt_l in range(4):
                    bt = g * 4 + bt_l
                    for Dt in range(2):
                        b_ = bt_l * 2 + Dt
                        tsl = OP('scalar', lambda h, p=bank(b_), o=QH[:, 0:512],
                                 sc=STATS[:, SC2 + bt_l:SC2 + bt_l + 1]:
                                 h.activation(o, p, AF.Silu, scale=sc),
                                 [tm_by_bank[b_]] + bank_dep(95))
                        set_bank(b_, tsl)
                        tst = ST(bass.AP(g1_t, bt * 128 * D + Dt * 512, [[D, 128], [1, 512]]),
                                 QH[:, 0:512], [tsl])
                        bank_tok[95] = tst
                        bank_tok[96] = tst
                        st_g1.append(tst)
            return st_g1

        l1ffn()
        P.barrier()

        tt1 = LD(STATS[0:1, 179:180], bass.AP(kv1_out_t, 2 * T * D, [[1, 1], [1, 1]]), [t_cc2])
        tt2 = LD(STATS[0:1, 180:181], bass.AP(kv1_out_t, KV_SHARD + 2 * T * D, [[1, 1], [1, 1]]), [t_cc2])
        OP('vector', lambda h: h.reduce_max(STATS[0:1, 181:182], STATS[0:1, 179:181],
                                            axis=AX.X), [tt1, tt2])
        tB1 = mk_bound(172, 181, BV_SELF)

        def epi_add(bt, Dt, pa, dd):
            xs = X[:, bt * 1024 + Dt * 512: bt * 1024 + (Dt + 1) * 512]
            return OP('vector', lambda h, p=pa, o=xs:
                      h.scalar_tensor_tensor(o, p, 1.0, o, ALU.bypass, ALU.add), dd)

        attention('self', kv1_out_t, KV_SHARD, qt1_t, apw, apb,
                  lambda kb: STATS[:, BV_SELF:BV_SELF + 1], epi_add, [t_cc2], bdeps=[tB1])
        P.barrier()

        tlnx = load_cols(lnxg, 0, 8, LNXG)
        tlnx += load_cols(lnxb, 0, 8, LNXB)
        tqab = load_cols(qab, 0, 8, 86, scale=0.125)
        ln_apply_to_fm(LNXG, LNXB, tlnx)
        P.barrier()
        proj_fm(qaw, 0, D, (0, 16384), mk_qt_writer(qtx_t),
                scale_col=EIGHTH, bias_col0=86, ncol=165, nscr=S1[:, 2048:2560],
                aux=tqab + [P.now('vector')])
        norm_fin(165, 173, [P.now('vector')])
        P.barrier()

        ttx = [LD(STATS[0:1, 184 + r:185 + r],
                  bass.AP(kvx_out_t, r * KVX_SHARD + 2 * 256 * D, [[1, 1], [1, 1]]), [t_cc3])
               for r in range(8)]
        OP('vector', lambda h: h.reduce_max(STATS[0:1, 168:169], STATS[0:1, 184:192],
                                            axis=AX.X), ttx)
        tsr = LD(QH[0:1, 0:2048], scorer_row, ch=11)
        tsrd = OP('vector', lambda h: h.reduce_max(STATS[0:1, 174:175], QH[0:1, 0:2048],
                                                   axis=AX.X), [tsr])
        tBx = mk_bound(173, 168, BV_CROSS, extra_col=174)
        tcv = []
        for kb in range(16):
            tcv.append(OP('vector', lambda h, o=STATS[:, CBV + kb:CBV + kb + 1],
                          s=STATS[:, 100 + kb:101 + kb], b_=STATS[:, BV_CROSS:BV_CROSS + 1]:
                          h.tensor_scalar(o, s, b_, None, ALU.add), [t_sc, tBx]))
        attention('cross', kvx_out_t, KVX_SHARD, qtx_t, xpw, xpb,
                  lambda kb: STATS[:, CBV + kb:CBV + kb + 1], epi_add, [t_cc3, tsrd],
                  bdeps=[tcv[-1]])
        P.barrier()

        tln2 = load_cols(ln2g, 0, 8, LN2G)
        tln2 += load_cols(ln2b, 0, 8, LN2B)
        ln_apply_to_fm(LN2G, LN2B, tln2)
        P.barrier()

        tfcb = load_cols(fcb, 0, 32, 96)

        def fc_writer(hc, tt_, pa, dd):
            o = QH[:, ts(tt_ & 1, 512)]
            tg = OP('scalar', lambda h, p=pa, o_=o:
                    h.activation(o_, p, AF.Gelu_apprx_tanh), dd + bank_dep(87 + (tt_ & 1)))
            tst = ST(bass.AP(hsp_t, hc * 128 * T + tt_ * 512, [[T, 128], [1, 512]]),
                     o, [tg])
            bank_tok[87 + (tt_ & 1)] = tst
            return tst

        proj_fm(fcw, 0, INNER, (0, 16384), fc_writer, bias_col0=96, aux=tfcb)
        P.barrier()

        tmb = LD(QH[0:1, 1024:2048], mpb, ch=9)
        acc_map = {}
        for Dt in range(2):
            for rnd in range(4):
                tws = [LD(WS[:, ts(c, 512)],
                          mpw[ts(rnd * 8 + c, 128), Dt * 512:(Dt + 1) * 512],
                          bank_dep(94), ch=8) for c in range(NC)]
                for bt in range(NBT):
                    sl = ts(bt & 1, 1024)
                    tl = None
                    for c_ in range(NC):
                        tl = LD(S1[:, (bt & 1) * 1024 + c_ * 128:(bt & 1) * 1024 + (c_ + 1) * 128],
                                bass.AP(hsp_t, (rnd * 8 + c_) * 128 * T + bt * 128,
                                        [[T, 128], [1, 128]]),
                                bank_dep(74 + (bt & 1)), ch=10 + (bt & 1))
                    b_ = 4 + (bt & 1)
                    pairs = [(S1[:, (bt & 1) * 1024 + c * 128:(bt & 1) * 1024 + (c + 1) * 128],
                              WS[:, ts(c, 512)]) for c in range(NC)]
                    last_rnd = (rnd == 3)
                    tm = mm_group(bank(b_), pairs, tws + [tl] + bank_dep(b_),
                                  start=True, stop=not last_rnd)
                    bank_tok[74 + (bt & 1)] = tm
                    bank_tok[94] = tm
                    if last_rnd:
                        tm = bias_mm(bank(b_), QH[0:1, 1024 + Dt * 512:1024 + (Dt + 1) * 512],
                                     [tm, tmb])
                    fs = FM[:, bt * 1024 + Dt * 512: bt * 1024 + (Dt + 1) * 512]
                    prev = acc_map.get((bt, Dt))
                    pdep = [prev] if prev else bank_dep(73)
                    if rnd == 0:
                        ta = OP('vector', lambda h, p=bank(b_), o=fs:
                                h.tensor_copy(o, p), [tm] + pdep)
                    else:
                        ta = OP('vector', lambda h, p=bank(b_), o=fs:
                                h.tensor_tensor(o, p, o, ALU.add), [tm] + pdep)
                    acc_map[(bt, Dt)] = ta
                    set_bank(b_, ta)
        for bt in range(NBT):
            for Dt in range(2):
                xs = X[:, bt * 1024 + Dt * 512: bt * 1024 + (Dt + 1) * 512]
                ta2 = OP('vector', lambda h, o=xs, a=FM[:, bt * 1024 + Dt * 512: bt * 1024 + (Dt + 1) * 512]:
                         h.tensor_add(o, o, a), [P.now('vector')])
                bank_tok[73] = ta2
        P.barrier()

        last_st = None
        for bt in range(NBT):
            tg = LD(S1[:, 0:1024], g1_t.ap()[ts(bt, 128), :], bank_dep(83), ch=10)
            tx = LD(S1[:, 1024:2048], x1_t.ap()[ts(bt, 128), :], bank_dep(83), ch=11)
            tm1 = OP('vector', lambda h, a=X[:, ts(bt, 1024)], g_=S1[:, 0:1024],
                     o=S1[:, 2048:3072]: h.tensor_mul(o, a, g_), [tg])
            tm2 = OP('vector', lambda h, a=S1[:, 2048:3072], b_=S1[:, 1024:2048],
                     o=S1[:, 3072:4096]: h.tensor_add(o, a, b_),
                     [tm1, tx] + bank_dep(76))
            bank_tok[83] = tm2
            last_st = ST(y[ts(bt, 128), :], S1[:, 3072:4096], [tm2])
            bank_tok[76] = last_st

        if DBG is not None:
            P.barrier()
            srcmap = {'qt0': qt0_t, 'qt1': qt1_t, 'qtx': qtx_t, 'g0': g0_t,
                      'g1': g1_t, 'x1sp': x1_t, 'hsp': hsp_t, 'kv0_in': kv0_in_t,
                      'kv0_out': kv0_out_t, 'kv1_in': kv1_in_t, 'kvx_in': kvx_in_t,
                      'kvx_out': kvx_out_t, 'w1qT': w1q_t, 'w2qT': w2q_t, 'xnfd': xnfd_t, 'xnfd1': xnfd_t, 'qt0e': xnfd_t}
            st_ = srcmap[DBG[0]]
            n_ = DBG[1]
            rows_ = n_ // 2048
            if rows_:
                ST(bass.AP(dbg_out, 0, [[2048, rows_], [1, 2048]]),
                   bass.AP(st_, 0, [[2048, rows_], [1, 2048]]))
            rem_ = n_ - rows_ * 2048
            if rem_:
                ST(bass.AP(dbg_out, rows_ * 2048, [[1, rem_]]),
                   bass.AP(st_, rows_ * 2048, [[1, rem_]]))

        P.wait('gpsimd', [(k, v) for k, v in P.cnt.items()
                          if k.startswith('d:gpsimd') and v])

        with nc.Block() as block:
            P.emit(block, sems)
    return nc, wshapes
